# version 03
# baseline (speedup 1.0000x reference)
import os
import subprocess
import sys
import tempfile
from contextlib import ExitStack
from math import comb

import numpy as np
import ml_dtypes

import concourse.bass as bass
import concourse.mybir as mybir
from concourse.bass_utils import run_bass_kernel_spmd

F32 = mybir.dt.float32
BF16 = mybir.dt.bfloat16
U8 = mybir.dt.uint8
AF = mybir.ActivationFunctionType
ALU = mybir.AluOpType
AX = mybir.AxisListType
BF = ml_dtypes.bfloat16

B, N, D = 2, 4096, 2048
H, KVH, HD = 16, 4, 128
INTER = 8192
N_CTRL = 130156
NSAMP = INTER * D
NCOL = NSAMP // 128
NSUB = NCOL // 512
NTILE = (NSUB + 2) // 3
OWN = 1024
EXP_SHIFT = -12.0
VW = 136

_CONST = {}
DEBUG = False


def _rope_tables():
    if "rope" in _CONST:
        return _CONST["rope"]
    prog = r"""
import jax
jax.config.update('jax_platforms', 'cpu')
jax.config.update('jax_enable_x64', True)
import jax.numpy as jnp, numpy as np, sys
HD, N, base = 128, 4096, 10000.0
inv = 1.0 / base ** (jnp.arange(0, HD, 2, dtype=jnp.float32) / HD)
t = jnp.arange(N, dtype=jnp.float32)
fr = jnp.outer(t, inv)
emb = jnp.concatenate([fr, fr], -1)
np.asarray(jnp.cos(emb)).astype(np.float32).tofile(sys.argv[1] + ".cos")
np.asarray(jnp.sin(emb)).astype(np.float32).tofile(sys.argv[1] + ".sin")
"""
    with tempfile.TemporaryDirectory() as td:
        pth = os.path.join(td, "rope.npz")
        env = dict(os.environ)
        env.pop("JAX_PLATFORMS", None)
        r = subprocess.run([sys.executable, "-c", prog, pth], env=env,
                           capture_output=True)
        if r.returncode != 0:
            raise RuntimeError(f"rope subprocess failed: {r.stderr[-2000:]}")
        cos = np.fromfile(pth + ".cos", dtype=np.float32).reshape(N, HD)
        sin = np.fromfile(pth + ".sin", dtype=np.float32).reshape(N, HD)
    ss = sin.copy()
    ss[:, :64] *= -1.0
    _CONST["rope"] = (cos, ss)
    return _CONST["rope"]


def _spline_packed():
    if "packed" in _CONST:
        return _CONST["packed"]
    t = np.linspace(0.0, N_CTRL - 1.0, NSAMP, dtype=np.float64)
    i = np.clip(np.floor(t).astype(np.int64), 0, N_CTRL - 2)
    i2 = i.reshape(NCOL, 128)
    t2 = t.reshape(NCOL, 128)
    i_c = i2[:, 0]
    b_c = (i2 == i_c[:, None]).sum(axis=1).astype(np.int64)
    phi = t2[:, 0] - i_c
    delta = (N_CTRL - 1.0) / (NSAMP - 1.0)
    tap_idx = np.clip(i_c[:, None] + np.arange(-1, 4)[None, :], 0, N_CTRL - 1)
    beta = np.array([[0.0, -0.5, 1.0, -0.5],
                     [1.0, 0.0, -2.5, 1.5],
                     [0.0, 0.5, 2.0, -1.5],
                     [0.0, 0.0, -0.5, 0.5]])
    pr = np.arange(128, dtype=np.float64)
    G = np.zeros((4, 4, 128))
    for k in range(4):
        for m in range(4):
            for q in range(m, 4):
                G[m, k] += beta[k, q] * comb(q, m) * (pr * delta) ** (q - m)
    G = G.reshape(16, 128)
    u_pow = phi[:, None] ** np.arange(4)[None, :]
    v_pow = (phi[:, None] - 1.0) ** np.arange(4)[None, :]
    U = np.zeros((NTILE, 128, 512), dtype=np.float64)
    TIDX = np.zeros((NSUB, 32, 512), dtype=np.int64)
    M = np.zeros((NSUB, 128, 512), dtype=np.uint8)
    pcol = np.arange(128)
    for gs in range(NSUB):
        ti, sl = gs // 3, gs % 3
        cols = np.arange(512 * gs, 512 * gs + 512)
        for m in range(4):
            for k in range(4):
                U[ti, 32 * sl + m * 4 + k, :] = u_pow[cols, m]
                U[ti, 32 * sl + 16 + m * 4 + k, :] = v_pow[cols, m]
                TIDX[gs, m * 4 + k, :] = tap_idx[cols, k]
                TIDX[gs, 16 + m * 4 + k, :] = tap_idx[cols, k + 1]
        M[gs] = (pcol[:, None] < b_c[cols][None, :]).astype(np.uint8)
    GST = np.zeros((128, 256), dtype=np.float64)
    for sl in range(3):
        GST[32 * sl:32 * sl + 16, 0:128] = G
        GST[32 * sl + 16:32 * sl + 32, 128:256] = G
    _CONST["packed"] = (U.astype(BF).reshape(NTILE * 128, 512),
                        M.reshape(NSUB * 128, 512), GST.astype(BF), TIDX)
    return _CONST["packed"]


def _core_inputs(inputs, b, c):
    U, M, GST, TIDX = _spline_packed()
    cos, ss = _rope_tables()
    x = np.asarray(inputs["L0_attn_x"][b])
    order = [(c + s) % 4 for s in range(4)]
    xp = np.ascontiguousarray(x.reshape(8, 4, 128, D)[:, order].reshape(N, D))
    cosp = cos.reshape(8, 4, 128, HD)[:, order].reshape(N, HD)
    ssp = ss.reshape(8, 4, 128, HD)[:, order].reshape(N, HD)
    coskr = np.ascontiguousarray(np.tile(cosp, (1, KVH)))
    sskr = np.ascontiguousarray(np.tile(ssp, (1, KVH)))
    own_rows = ((np.arange(8) * 512)[:, None] + 128 * c
                + np.arange(128)[None, :]).reshape(-1)
    s = np.float64(HD) ** -0.5
    cosqr = np.ascontiguousarray(np.tile((cos[own_rows].astype(np.float64) * s).astype(np.float32), (1, H)))
    ssqr = np.ascontiguousarray(np.tile((ss[own_rows].astype(np.float64) * s).astype(np.float32), (1, H)))
    tri = np.zeros((128, 512), dtype=np.float32)
    kk = np.arange(128)[:, None]
    pq = np.arange(128)[None, :]
    tri[:, 0:128] = np.where(kk <= pq, 0.0, -1e9)
    for s in range(1, 4):
        tri[:, 128 * s:128 * s + 128] = 0.0 if (c + s) % 4 < c else -1e9
    taps = {}
    for nm, key in (("tg", "L0_ffn_gate_cp"), ("tu", "L0_ffn_up_cp"),
                    ("td", "L0_ffn_down_cp")):
        cp = np.asarray(inputs[key])
        T = np.zeros((NTILE, 128, 512), dtype=BF)
        for gs in range(NSUB):
            T[gs // 3, 32 * (gs % 3):32 * (gs % 3) + 32, :] = cp[TIDX[gs]].astype(BF)
        taps[nm] = T.reshape(NTILE * 128, 512)
    return {
        "xp": xp, "coskr": coskr, "sskr": sskr, "cosqr": cosqr, "ssqr": ssqr,
        "tri_t": tri, "ident": np.eye(128, dtype=BF), "gst": GST,
        "uarr": U, "marr": M, "tg": taps["tg"], "tu": taps["tu"], "td": taps["td"],
        "wq": np.ascontiguousarray(inputs["L0_attn_Wq"]),
        "wk": np.ascontiguousarray(inputs["L0_attn_Wk"]),
        "wv": np.ascontiguousarray(inputs["L0_attn_Wv"]),
        "wo": np.ascontiguousarray(inputs["L0_attn_Wo"]),
    }


class Prog:
    def __init__(self):
        self.items = []
        self.cnt = {}
        self.sems = {}
        self.last_wait = {}

    def emit(self, eng, fn, inc=None, waits=()):
        for tok in waits:
            self._wait(eng, tok)
        tok_out = None
        if inc is not None:
            sem, amt = inc
            self.cnt[sem] = self.cnt.get(sem, 0) + amt
            tok_out = (sem, self.cnt[sem])

            def wrapped(h, fn=fn, sem=sem, amt=amt):
                fn(h).then_inc(self.sems[sem], amt)
        else:
            def wrapped(h, fn=fn):
                fn(h)
        self.items.append((eng, wrapped))
        return tok_out

    def _wait(self, eng, tok):
        if tok is None:
            return
        sem, val = tok
        if self.last_wait.get((eng, sem), -1) >= val:
            return
        self.last_wait[(eng, sem)] = val

        def w(h, sem=sem, val=val):
            h.wait_ge(self.sems[sem], val)
        self.items.append((eng, w))

    def all_toks(self):
        return [(s, v) for s, v in self.cnt.items()]

    def replay(self, eng, handle):
        for e, fn in self.items:
            if e == eng:
                fn(handle)


def build_program():
    nc = bass.Bass("TRN2", target_bir_lowering=False)
    d = {}

    def din(name, shape, dt):
        d[name] = nc.dram_tensor(name, shape, dt, kind="ExternalInput").ap()

    din("xp", [N, D], F32)
    din("coskr", [N, 512], F32)
    din("sskr", [N, 512], F32)
    din("cosqr", [OWN, D], F32)
    din("ssqr", [OWN, D], F32)
    din("tri_t", [128, 512], F32)
    din("ident", [128, 128], BF16)
    din("gst", [128, 256], BF16)
    din("uarr", [NTILE * 128, 512], BF16)
    din("marr", [NSUB * 128, 512], U8)
    din("tg", [NTILE * 128, 512], BF16)
    din("tu", [NTILE * 128, 512], BF16)
    din("td", [NTILE * 128, 512], BF16)
    din("wq", [D, D], F32)
    din("wk", [D, 512], F32)
    din("wv", [D, 512], F32)
    din("wo", [D, D], F32)
    out_h = nc.dram_tensor("out", [OWN, D], F32, kind="ExternalOutput").ap()
    dbg = {}
    if DEBUG:
        for nm, shp, dt in (("dbg_kfm", [128, KVH * N], BF16),
                            ("dbg_vtm", [128, KVH * 32 * VW], BF16),
                            ("dbg_qfm", [128, H * OWN], BF16),
                            ("dbg_atm", [128, 8 * D], BF16),
                            ("dbg_ln2fm", [128, 16 * OWN], BF16),
                            ("dbg_inter", [128, 65536], BF16),
                            ("dbg_fmb", [128, 8192], BF16)):
            dbg[nm] = nc.dram_tensor(nm, shp, dt, kind="ExternalOutput").ap()
    _scr_kind = "ExternalOutput" if DEBUG else "Internal"
    wd_hbm = nc.dram_tensor("wd_hbm", [128, NCOL], BF16, kind=_scr_kind).ap()
    o_hbm = nc.dram_tensor("o_hbm", [OWN, D], F32, kind=_scr_kind).ap()
    x1_hbm = nc.dram_tensor("x1_hbm", [OWN, D], F32, kind=_scr_kind).ap()
    ffn_hbm = nc.dram_tensor("ffn_hbm", [D, OWN], BF16, kind=_scr_kind).ap()

    es = ExitStack()
    ABF = es.enter_context(nc.sbuf_tensor("abf", [128, 96256], BF16)).ap()
    AF32 = es.enter_context(nc.sbuf_tensor("af32", [128, 4736], F32)).ap()
    AU8 = es.enter_context(nc.sbuf_tensor("au8", [128, 1024], U8)).ap()
    ps_f = [es.enter_context(nc.psum_tensor(f"psf{i}", [128, 512], F32)).ap()
            for i in range(6)]
    ps_b = [es.enter_context(nc.psum_tensor(f"psb{i}", [128, 1024], BF16)).ap()
            for i in range(2)]

    KFM = ABF[:, 0:16384]
    VTM = ABF[:, 16384:33792]
    WKV = ABF[:, 33792:50176]
    QFM = ABF[:, 33792:50176]
    AFM = ABF[:, 33792:50176]
    FMB = ABF[:, 50176:58368]
    WQ4 = ABF[:, 50176:58368]
    LNT = ABF[:, 58368:60416]
    WOH = ABF[:, 50176:66560]
    OWNFM = ABF[:, 66560:82944]
    ATM = ABF[:, 66560:82944]
    LN2FM = ABF[:, 66560:82944]
    WDRD = [ABF[:, 66560:74752], ABF[:, 74752:82944]]
    DSTG = ABF[:, 0:8192]
    INTER_SB = ABF[:, 0:65536]
    CKT = [ABF[:, 82944 + i * 1024:82944 + (i + 1) * 1024].bitcast(F32) for i in range(2)]
    SKT = [ABF[:, 84992 + i * 1024:84992 + (i + 1) * 1024].bitcast(F32) for i in range(2)]
    KSW = ABF[:, 87040:88064].bitcast(F32)
    KRP = ABF[:, 88064:89088].bitcast(F32)
    KPB = ABF[:, 89088:89600]
    PTB = [ABF[:, 82944:83456], ABF[:, 83456:83968]]
    TT = [ABF[:, 82944 + i * 512:82944 + (i + 1) * 512] for i in range(2)]
    UT = [ABF[:, 83968 + i * 512:83968 + (i + 1) * 512] for i in range(2)]
    F12T = [ABF[:, 84992 + i * 512:84992 + (i + 1) * 512] for i in range(2)]
    WGB = [ABF[:, 86016 + i * 2048:86016 + (i + 1) * 2048] for i in range(2)]
    WUB = [ABF[:, 90112 + i * 2048:90112 + (i + 1) * 2048] for i in range(2)]
    DEV = [ABF[:, 94208 + i * 512:94208 + (i + 1) * 512] for i in range(2)]
    SIL = ABF[:, 95616:96128]
    X1T = ABF[:, 8192:12288].bitcast(F32)
    X1T2 = ABF[:, 12288:16384].bitcast(F32)
    EV4 = [X1T, X1T2]
    FT8 = ABF[:, 16384:18432]
    X8A = ABF[:, 18432:22528].bitcast(F32)
    X8B = ABF[:, 22528:26624].bitcast(F32)
    GSTS = ABF[:, 95232:95488]
    IDENT = ABF[:, 95488:95616]
    XB = [AF32[:, 0:2048], AF32[:, 2048:4096]]
    TRI = AF32[:, 4096:4608]
    STATS = AF32[:, 4608:4736]
    sc_sum = STATS[:, 0:1]
    sc_ssq = STATS[:, 1:2]
    sc_mean = STATS[:, 2:3]
    sc_var = STATS[:, 3:4]
    sc_r1 = STATS[:, 4:5]
    sc_rstd = STATS[:, 5:6]
    sc_nb = STATS[:, 6:7]
    sc_t = STATS[:, 7:8]
    sc_rs = [STATS[:, 8 + i:9 + i] for i in range(2)]
    sc_nm5 = STATS[:, 10:11]
    sc_shift = STATS[:, 11:12]
    MSK = [AU8[:, 0:512], AU8[:, 512:1024]]

    p = Prog()
    for s in ("te", "dv", "ac"):
        p.sems[s] = es.enter_context(nc.semaphore(s))
    _slot_sems = {}

    def _slot(name):
        if name not in _slot_sems:
            p.sems["s_" + name] = es.enter_context(nc.semaphore("s_" + name))
            _slot_sems[name] = "s_" + name
        return _slot_sems[name]

    E_SY, E_TE, E_DV, E_AC = "sync", "tensor", "vector", "scalar"

    def dma(dst, src, waits=(), slot="misc"):
        return p.emit(E_SY, lambda h, a=dst, b=src: h.dma_start(out=a, in_=b),
                      inc=(_slot(name=slot), 16), waits=waits)

    def dma_out(dst, src, waits=(), slot="out"):
        return p.emit(E_SY, lambda h, a=dst, b=src: h.dma_start(out=a, in_=b),
                      inc=(_slot(name=slot), 16), waits=waits)

    def te(fn, waits=()):
        return p.emit(E_TE, fn, inc=("te", 1), waits=waits)

    def dv(fn, waits=()):
        return p.emit(E_DV, fn, inc=("dv", 1), waits=waits)

    def ac(fn, waits=()):
        return p.emit(E_AC, fn, inc=("ac", 1), waits=waits)

    def barrier():
        toks = p.all_toks()
        for eng in (E_SY, E_TE, E_DV, E_AC):
            for t in toks:
                p._wait(eng, t)

    psb_free = [None, None]

    def transpose_evac(src_tile, dst_tile, bank, waits):
        w = list(waits) + [t_id]
        if psb_free[bank] is not None:
            w.append(psb_free[bank])
        tr = te(lambda h, b=bank, s=src_tile:
                h.transpose(ps_b[b][:, 0:128], s, IDENT), waits=w)
        ev = dv(lambda h, b=bank, o=dst_tile: h.tensor_copy(o, ps_b[b][:, 0:128]),
                waits=[tr])
        psb_free[bank] = ev
        return tr, ev

    t_id = dma(IDENT, d["ident"][:], slot="ident")
    t_gst = dma(GSTS, d["gst"][:], slot="gst")
    t_tri = dma(TRI, d["tri_t"][:], slot="tri")
    t_shift = dv(lambda h: h.memset(sc_shift, EXP_SHIFT))
    vt_v = VTM.rearrange("p (t w) -> p t w", w=VW)
    z0 = dv(lambda h: h.memset(vt_v[:, :, 128:VW], 0.0))
    t_ones = dv(lambda h: h.memset(vt_v[:, :, 128:129], 1.0), waits=[z0])

    prev = None
    for dc in range(16):
        w = [prev] if prev else []
        tl = dma(XB[dc % 2][:, 0:512], d["wk"][128 * dc:128 * dc + 128, :], waits=w, slot=f"xb{dc % 2}")
        dv(lambda h, dc=dc: h.tensor_copy(WKV[:, dc * 512:dc * 512 + 512],
                                          XB[dc % 2][:, 0:512]), waits=[tl])
        tl2 = dma(XB[dc % 2][:, 512:1024], d["wv"][128 * dc:128 * dc + 128, :], waits=[tl], slot=f"xb{dc % 2}")
        prev = dv(lambda h, dc=dc: h.tensor_copy(
            WKV[:, 8192 + dc * 512:8192 + dc * 512 + 512],
            XB[dc % 2][:, 512:1024]), waits=[tl2])
    t_wkv = prev

    def ln_tile(xt, src_rows, prev_apply=None, lnt_guard=None, extra=()):
        tl = dma(xt, src_rows, waits=[x for x in (prev_apply,) if x] + list(extra))
        w0 = [tl] + [x for x in (prev_apply, lnt_guard) if x]
        s1 = dv(lambda h: h.reduce_sum(sc_sum, xt, axis=AX.X), waits=w0)
        s2 = ac(lambda h: h.activation(LNT, xt, AF.Square, accum_out=sc_ssq), waits=w0)
        s3 = dv(lambda h: h.tensor_scalar_mul(sc_mean, sc_sum, 1.0 / D), waits=[s1])
        s4 = dv(lambda h: h.tensor_scalar_mul(sc_var, sc_ssq, 1.0 / D), waits=[s2])
        s5 = dv(lambda h: h.tensor_mul(sc_t, sc_mean, sc_mean), waits=[s3])
        s6 = dv(lambda h: h.tensor_sub(sc_var, sc_var, sc_t), waits=[s4, s5])
        s7 = dv(lambda h: h.tensor_scalar_add(sc_var, sc_var, 1e-5), waits=[s6])
        s8 = dv(lambda h: h.reciprocal(sc_r1, sc_var), waits=[s7])
        s9 = ac(lambda h: h.activation(sc_rstd, sc_r1, AF.Sqrt), waits=[s8])
        s10 = dv(lambda h: h.tensor_scalar_mul(sc_nb, sc_mean, -1.0), waits=[s9])
        s11 = dv(lambda h: h.tensor_scalar(LNT, xt, sc_nb, sc_rstd,
                                           ALU.add, ALU.mult), waits=[s10])
        return tl, s11

    prev_apply = None
    last_tr = None
    fm_consumed = None
    kfm_done = []
    vtm_done = []
    ckt_free = [None, None]
    skt_free = [None, None]
    kpb_free = None
    for tt in range(32):
        pc, sub = tt // 4, tt % 4
        xt = XB[tt % 2]
        tl, app = ln_tile(xt, d["xp"][128 * tt:128 * tt + 128, :],
                          prev_apply=prev_apply, lnt_guard=last_tr,
                          extra=([t_wkv] if tt < 2 else ()))
        prev_apply = app
        evs = []
        for dc in range(16):
            w = [app]
            if fm_consumed is not None:
                w.append(fm_consumed)
            tr, ev = transpose_evac(
                LNT[:, dc * 128:dc * 128 + 128],
                FMB[:, dc * 512 + 128 * sub:dc * 512 + 128 * sub + 128],
                dc % 2, w)
            evs.append(ev)
            last_tr = tr
        if sub < 3:
            continue
        for st in range(4):
            gt = 4 * pc + st
            pk = ps_f[2 * (st % 2)]
            pv = ps_f[2 * (st % 2) + 1]
            mm_last = None
            for dc in range(16):
                if dc == 0:
                    w = [t_wkv, evs[-1]]
                    if kfm_done:
                        w.append(kfm_done[-1])
                    if vtm_done:
                        w.append(vtm_done[-1])
                else:
                    w = []
                te(lambda h, pk=pk, dc=dc, st=st:
                   h.matmul(pk[:], FMB[:, dc * 512 + 128 * st:dc * 512 + 128 * st + 128],
                            WKV[:, dc * 512:dc * 512 + 512],
                            start=(dc == 0), stop=(dc == 15)), waits=w)
                mm_last = te(lambda h, pv=pv, dc=dc, st=st:
                             h.matmul(pv[:], FMB[:, dc * 512 + 128 * st:dc * 512 + 128 * st + 128],
                                      WKV[:, 8192 + dc * 512:8192 + dc * 512 + 512],
                                      start=(dc == 0), stop=(dc == 15)), waits=[])
            ck = dma(CKT[st % 2], d["coskr"][128 * gt:128 * gt + 128, :],
                     waits=[ckt_free[st % 2]] if ckt_free[st % 2] else [], slot=f"ck{st % 2}")
            sk = dma(SKT[st % 2], d["sskr"][128 * gt:128 * gt + 128, :],
                     waits=[skt_free[st % 2]] if skt_free[st % 2] else [], slot=f"sk{st % 2}")
            kv3 = lambda ap: ap.rearrange("p (g h w) -> p (g h) w", h=2, w=64)
            r1 = dv(lambda h, pk=pk: h.tensor_copy(
                kv3(KSW)[:, 0::2, :], kv3(pk[:])[:, 1::2, :]), waits=[mm_last])
            r2 = dv(lambda h, pk=pk: h.tensor_copy(
                kv3(KSW)[:, 1::2, :], kv3(pk[:])[:, 0::2, :]), waits=[r1])
            r3 = dv(lambda h, st=st: h.tensor_mul(KRP, KSW, SKT[st % 2]),
                    waits=[r2, sk])
            r4 = dv(lambda h, pk=pk, st=st: h.tensor_mul(KSW, pk[:], CKT[st % 2]),
                    waits=[ck, r3])
            skt_free[st % 2] = r3
            ckt_free[st % 2] = r4
            w5 = [r4]
            if kpb_free is not None:
                w5.append(kpb_free)
            r5 = dv(lambda h: h.tensor_add(KPB, KSW, KRP), waits=w5)
            for g in range(KVH):
                trk, evk = transpose_evac(
                    KPB[:, g * 128:g * 128 + 128],
                    KFM[:, g * 4096 + 128 * gt:g * 4096 + 128 * gt + 128],
                    g % 2, [r5])
                kfm_done.append(evk)
                kpb_free = trk
                evv = ac(lambda h, pv=pv, g=g, gt=gt:
                         h.activation(VTM[:, g * 32 * VW + gt * VW:g * 32 * VW + gt * VW + 128],
                                      pv[:, g * 128:g * 128 + 128], AF.Copy),
                         waits=[mm_last, t_ones])
                vtm_done.append(evv)
        fm_consumed = kfm_done[-1]

    barrier()
    if DEBUG:
        dma_out(dbg["dbg_kfm"], KFM)
        dma_out(dbg["dbg_vtm"], VTM)
        dma_out(dbg["dbg_fmb"], FMB)
        barrier()

    prev_apply = None
    last_tr = None
    own_evs = []
    for tt in range(8):
        xt = XB[tt % 2]
        tl, app = ln_tile(xt, d["xp"][512 * tt:512 * tt + 128, :],
                          prev_apply=prev_apply, lnt_guard=last_tr)
        prev_apply = app
        for dc in range(16):
            tr, ev = transpose_evac(
                LNT[:, dc * 128:dc * 128 + 128],
                OWNFM[:, dc * 1024 + tt * 128:dc * 1024 + tt * 128 + 128],
                dc % 2, [app])
            own_evs.append(ev)
            last_tr = tr
    t_own = own_evs[-1]

    qfm_done = []
    wq_c = []
    kpb_free = None
    for hq in range(4):
        for dc in range(16):
            w = []
            if len(wq_c) >= 2:
                w.append(wq_c[-2])
            elif prev_apply is not None:
                w.append(prev_apply)
            if hq > 0:
                w.append(("te", p.cnt["te"]))
            tl = dma(XB[dc % 2][:, 0:512],
                     d["wq"][128 * dc:128 * dc + 128, 512 * hq:512 * hq + 512],
                     waits=w, slot=f"xb{dc % 2}")
            cc = dv(lambda h, dc=dc: h.tensor_copy(WQ4[:, dc * 512:dc * 512 + 512],
                                                   XB[dc % 2][:, 0:512]), waits=[tl])
            wq_c.append(cc)
        for tt in range(8):
            pq = ps_f[tt % 2]
            mm_last = None
            for dc in range(16):
                if dc == 0:
                    w = [wq_c[-1], t_own]
                    if len(qfm_done) >= 8:
                        w.append(qfm_done[-8])
                else:
                    w = []
                mm_last = te(lambda h, pq=pq, dc=dc, tt=tt:
                             h.matmul(pq[:], OWNFM[:, dc * 1024 + tt * 128:dc * 1024 + tt * 128 + 128],
                                      WQ4[:, dc * 512:dc * 512 + 512],
                                      start=(dc == 0), stop=(dc == 15)), waits=w)
            ck = dma(CKT[tt % 2], d["cosqr"][128 * tt:128 * tt + 128, 512 * hq:512 * hq + 512],
                     waits=[ckt_free[tt % 2]] if ckt_free[tt % 2] else [], slot=f"ck{tt % 2}")
            sk = dma(SKT[tt % 2], d["ssqr"][128 * tt:128 * tt + 128, 512 * hq:512 * hq + 512],
                     waits=[skt_free[tt % 2]] if skt_free[tt % 2] else [], slot=f"sk{tt % 2}")
            kv3 = lambda ap: ap.rearrange("p (g h w) -> p (g h) w", h=2, w=64)
            r1 = dv(lambda h, pq=pq: h.tensor_copy(
                kv3(KSW)[:, 0::2, :], kv3(pq[:])[:, 1::2, :]), waits=[mm_last])
            r2 = dv(lambda h, pq=pq: h.tensor_copy(
                kv3(KSW)[:, 1::2, :], kv3(pq[:])[:, 0::2, :]), waits=[r1])
            r3 = dv(lambda h, tt=tt: h.tensor_mul(KRP, KSW, SKT[tt % 2]),
                    waits=[r2, sk])
            r4 = dv(lambda h, pq=pq, tt=tt: h.tensor_mul(KSW, pq[:], CKT[tt % 2]),
                    waits=[ck, r3])
            skt_free[tt % 2] = r3
            ckt_free[tt % 2] = r4
            w5 = [r4]
            if kpb_free is not None:
                w5.append(kpb_free)
            r5 = dv(lambda h: h.tensor_add(KPB, KSW, KRP), waits=w5)
            for gg in range(4):
                hh = 4 * hq + gg
                trq, evq = transpose_evac(
                    KPB[:, gg * 128:gg * 128 + 128],
                    QFM[:, hh * 1024 + tt * 128:hh * 1024 + tt * 128 + 128],
                    gg % 2, [r5])
                qfm_done.append(evq)
                kpb_free = trq

    barrier()
    if DEBUG:
        dma_out(dbg["dbg_qfm"], QFM)
        barrier()

    def load_wo_half(dh, gate_tok):
        prev_c = gate_tok
        for acx in range(16):
            t1 = dma(XB[acx % 2][:, 0:1024],
                     d["wo"][128 * acx:128 * acx + 128, 1024 * dh:1024 * dh + 1024],
                     waits=[prev_c] if prev_c else [], slot=f"xb{acx % 2}")
            prev_c = dv(lambda h, acx=acx: h.tensor_copy(
                WOH[:, acx * 1024:acx * 1024 + 1024], XB[acx % 2][:, 0:1024]),
                waits=[t1])
        return prev_c

    t_woh = load_wo_half(0, None)

    atm_done = []
    pt_free = [None, None]
    st_free = [None, None]
    av_free = [None, None]
    rs_free = [None, None]
    t_q = qfm_done[-1]
    t_k = kfm_done[-1]
    t_v = vtm_done[-1]
    for hh in range(H):
        g = hh // 4
        for qt in range(8):
            avb = ps_f[2 + (qt % 2)]
            av_last = None
            for kb in range(qt + 1):
                sbk = ps_f[4 + (kb % 2)]
                mms = []
                for ci in range(4):
                    if ci == 0:
                        w = [t_q, t_k]
                        if st_free[kb % 2] is not None:
                            w.append(st_free[kb % 2])
                    else:
                        w = []
                    mms.append(te(lambda h, sbk=sbk, g=g, kb=kb, ci=ci, hh=hh, qt=qt:
                                  h.matmul(sbk[:, 128 * ci:128 * ci + 128],
                                           KFM[:, g * 4096 + 512 * kb + 128 * ci:
                                               g * 4096 + 512 * kb + 128 * ci + 128],
                                           QFM[:, hh * 1024 + qt * 128:hh * 1024 + qt * 128 + 128],
                                           start=True, stop=True), waits=w))
                if kb == qt:
                    mk = dv(lambda h, sbk=sbk: h.tensor_add(sbk[:], sbk[:], TRI),
                            waits=mms + [t_tri])
                    exp_w = [mk]
                else:
                    exp_w = [mms[-1]]
                if pt_free[kb % 2] is not None:
                    exp_w = exp_w + [pt_free[kb % 2]]
                ex = ac(lambda h, sbk=sbk, kb=kb:
                        h.activation(PTB[kb % 2], sbk[:], AF.Exp, bias=sc_shift),
                        waits=exp_w + [t_shift])
                st_free[kb % 2] = ex
                for ci in range(4):
                    w = [ex]
                    if kb == 0 and ci == 0:
                        w.append(t_v)
                        if av_free[qt % 2] is not None:
                            w.append(av_free[qt % 2])
                    av_last = te(lambda h, avb=avb, g=g, kb=kb, ci=ci:
                                 h.matmul(avb[:, 0:VW],
                                          PTB[kb % 2][:, 128 * ci:128 * ci + 128],
                                          VTM[:, g * 32 * VW + (4 * kb + ci) * VW:
                                              g * 32 * VW + (4 * kb + ci) * VW + VW],
                                          start=(kb == 0 and ci == 0),
                                          stop=(kb == qt and ci == 3)), waits=w)
                pt_free[kb % 2] = av_last
            rsl = sc_rs[qt % 2]
            w = [av_last]
            if rs_free[qt % 2] is not None:
                w.append(rs_free[qt % 2])
            rc = dv(lambda h, avb=avb, rsl=rsl: h.reciprocal(rsl, avb[:, 128:129]),
                    waits=w)
            evo = dv(lambda h, avb=avb, rsl=rsl, hh=hh, qt=qt:
                     h.tensor_scalar_mul(ATM[:, qt * 2048 + hh * 128:qt * 2048 + hh * 128 + 128],
                                         avb[:, 0:128], rsl), waits=[rc])
            rs_free[qt % 2] = evo
            av_free[qt % 2] = evo
            atm_done.append(evo)

    if DEBUG:
        dma_out(dbg["dbg_atm"], ATM, waits=[atm_done[-1]])
    barrier()

    def recon_sub(sl, fslot, f12_tok, sel_view, sel_waits, psum_guard, msk):
        w1w = [f12_tok, t_gst]
        if psum_guard is not None:
            w1w.append(psum_guard)
        w1 = te(lambda h, sl=sl, fslot=fslot:
                h.matmul(ps_f[4][:], GSTS[32 * sl:32 * sl + 32, 0:128],
                         F12T[fslot][32 * sl:32 * sl + 32, :],
                         start=True, stop=True, tile_position=(32 * sl, 0)),
                waits=w1w)
        te(lambda h, sl=sl, fslot=fslot:
           h.matmul(ps_f[5][:], GSTS[32 * sl:32 * sl + 32, 128:256],
                    F12T[fslot][32 * sl:32 * sl + 32, :],
                    start=True, stop=True, tile_position=(32 * sl, 0)), waits=[])
        t_w2 = ("te", p.cnt["te"])
        sel = dv(lambda h, v=sel_view, m=msk:
                 h.select(v, m, ps_f[4][:], ps_f[5][:]),
                 waits=[t_w2] + list(sel_waits))
        return sel

    down_sel = None
    stg_dma = None
    f12_tok = None
    for gs in range(NSUB):
        hc = gs // 16
        hl = 8 * (gs % 16)
        ti, parf = gs // 3, (gs // 3) % 2
        if gs % 3 == 0:
            wprev = [down_sel] if down_sel else []
            ut_l = dma(UT[parf], d["uarr"][128 * ti:128 * ti + 128, :], waits=wprev, slot=f"ut{parf}")
            tt_l = dma(TT[parf], d["td"][128 * ti:128 * ti + 128, :], waits=wprev, slot=f"tt{parf}")
            f12_tok = dv(lambda h, parf=parf: h.tensor_mul(F12T[parf], UT[parf], TT[parf]),
                         waits=[ut_l, tt_l])
        mtok = dma(MSK[gs % 2], d["marr"][128 * gs:128 * gs + 128, :],
                   waits=[down_sel] if down_sel else [], slot=f"msk{gs % 2}")
        sel_w = [mtok]
        if stg_dma is not None and gs % 16 == 0:
            sel_w.append(stg_dma)
        view = DSTG.rearrange("p (ic hl) -> p hl ic", hl=128)[:, hl:hl + 8, :]
        down_sel = recon_sub(gs % 3, parf, f12_tok, view, sel_w, down_sel, MSK[gs % 2])
        if gs % 16 == 15:
            stg_dma = dma_out(wd_hbm[:, hc * 8192:hc * 8192 + 8192], DSTG,
                              waits=[down_sel], slot="stg")
    t_wd_written = stg_dma

    afm_done = []
    t_atm = atm_done[-1]
    for qt in range(8):
        for acx in range(16):
            tr, ev = transpose_evac(
                ATM[:, qt * 2048 + acx * 128:qt * 2048 + acx * 128 + 128],
                AFM[:, (qt * 16 + acx) * 128:(qt * 16 + acx) * 128 + 128],
                acx % 2, [t_atm])
            afm_done.append(ev)
    t_afm = afm_done[-1]

    o_free = [None] * 4
    for dh in range(2):
        if dh == 1:
            t_woh = load_wo_half(1, ("te", p.cnt["te"]))
        for qt in range(8):
            pso = [ps_f[2 * (qt % 2)], ps_f[2 * (qt % 2) + 1]]
            mm_last = None
            for acx in range(16):
                for db in range(2):
                    if acx == 0 and db == 0:
                        w = [t_afm, t_woh]
                        for b in (0, 1):
                            if o_free[2 * (qt % 2) + b] is not None:
                                w.append(o_free[2 * (qt % 2) + b])
                    else:
                        w = []
                    mm_last = te(lambda h, pso=pso, qt=qt, acx=acx, db=db:
                                 h.matmul(pso[db][:],
                                          AFM[:, (qt * 16 + acx) * 128:(qt * 16 + acx) * 128 + 128],
                                          WOH[:, acx * 1024 + 512 * db:acx * 1024 + 512 * db + 512],
                                          start=(acx == 0), stop=(acx == 15)), waits=w)
            for db in range(2):
                evd = dv(lambda h, pso=pso, db=db, qt=qt:
                         h.tensor_copy(EV4[qt % 2][:, 512 * db:512 * db + 512], pso[db][:]),
                         waits=[mm_last])
                od = dma_out(o_hbm[128 * qt:128 * qt + 128,
                                   1024 * dh + 512 * db:1024 * dh + 512 * db + 512],
                             EV4[qt % 2][:, 512 * db:512 * db + 512], waits=[evd],
                             slot=f"oev{2 * (qt % 2) + db}")
                o_free[2 * (qt % 2) + db] = od

    barrier()

    def ln_stats(src, w0):
        s1 = dv(lambda h: h.reduce_sum(sc_sum, src, axis=AX.X), waits=w0)
        s2 = ac(lambda h: h.activation(LNT, src, AF.Square, accum_out=sc_ssq), waits=w0)
        s3 = dv(lambda h: h.tensor_scalar_mul(sc_mean, sc_sum, 1.0 / D), waits=[s1])
        s4 = dv(lambda h: h.tensor_scalar_mul(sc_var, sc_ssq, 1.0 / D), waits=[s2])
        s5 = dv(lambda h: h.tensor_mul(sc_t, sc_mean, sc_mean), waits=[s3])
        s6 = dv(lambda h: h.tensor_sub(sc_var, sc_var, sc_t), waits=[s4, s5])
        s7 = dv(lambda h: h.tensor_scalar_add(sc_var, sc_var, 1e-5), waits=[s6])
        s8 = dv(lambda h: h.reciprocal(sc_r1, sc_var), waits=[s7])
        s9 = ac(lambda h: h.activation(sc_rstd, sc_r1, AF.Sqrt), waits=[s8])
        s10a = dv(lambda h: h.scalar_tensor_tensor(sc_nb, sc_mean, -1.0, sc_rstd,
                                                   ALU.mult, ALU.mult), waits=[s9])
        s10 = dv(lambda h: h.tensor_scalar_mul(sc_nm5, sc_mean, -1.0), waits=[s10a])
        return s10

    prev_end = None
    last_tr = None
    ln2_done = []
    for qt in range(8):
        w = [prev_end] if prev_end else []
        to = dma(XB[0], o_hbm[128 * qt:128 * qt + 128, :], waits=w, slot="xb0")
        tx = dma(XB[1], d["xp"][512 * qt:512 * qt + 128, :], waits=w, slot="xb1")
        w0 = [to] + ([prev_end] if prev_end else []) + ([last_tr] if last_tr else [])
        s10 = ln_stats(XB[0], w0)
        a1 = dv(lambda h: h.tensor_scalar(X1T, XB[0], sc_rstd, sc_nb,
                                          ALU.mult, ALU.add), waits=[s10])
        a2 = dv(lambda h: h.tensor_add(X1T2, X1T, XB[1]), waits=[a1, tx])
        sp = dma_out(x1_hbm[128 * qt:128 * qt + 128, :], X1T2, waits=[a2], slot="x1sp")
        s10b = ln_stats(X1T2, [a2])
        appb = dv(lambda h: h.tensor_scalar(LNT, X1T2, sc_nm5, sc_rstd,
                                            ALU.add, ALU.mult), waits=[s10b, sp])
        prev_end = appb
        for dc in range(16):
            tr, ev = transpose_evac(
                LNT[:, dc * 128:dc * 128 + 128],
                LN2FM[:, dc * 1024 + qt * 128:dc * 1024 + qt * 128 + 128],
                dc % 2, [appb])
            ln2_done.append(ev)
            last_tr = tr
        prev_end = ev
    t_ln2 = ln2_done[-1]

    if DEBUG:
        dma_out(dbg["dbg_ln2fm"], LN2FM, waits=[t_ln2])
    barrier()

    gsel, usel = None, None
    f12g, f12u = None, None
    cur_tile = -1
    gu_free = [None, None]
    silu_toks = []
    inter_done = []
    for ib in range(64):
        par_b = ib % 2
        for s4 in range(4):
            gs = 4 * ib + s4
            ti, parf = gs // 3, (gs // 3) % 2
            if ti != cur_tile:
                cur_tile = ti
                wprev = [x for x in (gsel, usel) if x is not None]
                ut_l = dma(UT[parf], d["uarr"][128 * ti:128 * ti + 128, :], waits=wprev, slot=f"ut{parf}")
                tg_l = dma(TT[parf], d["tg"][128 * ti:128 * ti + 128, :], waits=wprev, slot=f"tt{parf}")
                f12g = dv(lambda h, parf=parf: h.tensor_mul(F12T[parf], UT[parf], TT[parf]),
                          waits=[ut_l, tg_l])
                tu_l = dma(TT[1 - parf], d["tu"][128 * ti:128 * ti + 128, :], waits=wprev, slot=f"tt{1 - parf}")
                f12u = dv(lambda h, parf=parf:
                          h.tensor_mul(F12T[1 - parf], UT[parf], TT[1 - parf]),
                          waits=[ut_l, tu_l, f12g])
            mtok = dma(MSK[gs % 2], d["marr"][128 * gs:128 * gs + 128, :],
                       waits=[x for x in (gsel, usel) if x is not None], slot=f"msk{gs % 2}")
            extra = [mtok]
            if s4 == 0 and gu_free[par_b] is not None:
                extra.append(gu_free[par_b])
            gview = WGB[par_b].rearrange("p (dc i) -> p i dc", i=128)[:, 32 * s4:32 * s4 + 32, :]
            gsel = recon_sub(gs % 3, parf, f12g, gview, extra, usel, MSK[gs % 2])
            uview = WUB[par_b].rearrange("p (dc i) -> p i dc", i=128)[:, 32 * s4:32 * s4 + 32, :]
            usel = recon_sub(gs % 3, 1 - parf, f12u, uview, extra, gsel, MSK[gs % 2])
        mm_g_last, mm_u_last = None, None
        for dc in range(16):
            for th in range(2):
                if dc == 0 and th == 0:
                    w = [gsel, usel, t_ln2]
                    if silu_toks:
                        w.append(silu_toks[-1])
                    if inter_done:
                        w.append(inter_done[-1])
                else:
                    w = []
                mm_g_last = te(lambda h, dc=dc, th=th, par_b=par_b:
                               h.matmul(ps_f[th][:], WGB[par_b][:, dc * 128:dc * 128 + 128],
                                        LN2FM[:, dc * 1024 + 512 * th:dc * 1024 + 512 * th + 512],
                                        start=(dc == 0), stop=(dc == 15)), waits=w)
            for th in range(2):
                mm_u_last = te(lambda h, dc=dc, th=th, par_b=par_b:
                               h.matmul(ps_f[2 + th][:], WUB[par_b][:, dc * 128:dc * 128 + 128],
                                        LN2FM[:, dc * 1024 + 512 * th:dc * 1024 + 512 * th + 512],
                                        start=(dc == 0), stop=(dc == 15)), waits=[])
        gu_free[par_b] = mm_u_last
        for th in range(2):
            w = [mm_g_last, mm_u_last]
            if inter_done:
                w.append(inter_done[-1])
            sl1 = ac(lambda h, th=th: h.activation(SIL, ps_f[th][:], AF.Silu), waits=w)
            silu_toks.append(sl1)
            mlt = dv(lambda h, th=th, ib=ib:
                     h.tensor_mul(INTER_SB[:, ib * 1024 + 512 * th:ib * 1024 + 512 * th + 512],
                                  SIL, ps_f[2 + th][:]), waits=[sl1])
            inter_done.append(mlt)
    t_inter = inter_done[-1]

    if DEBUG:
        dma_out(dbg["dbg_inter"], INTER_SB, waits=[t_inter])
    barrier()

    wd_free = [None, None]
    dev_free = [None, None]
    for hc in range(16):
        w = [t_wd_written]
        if wd_free[hc % 2] is not None:
            w.append(wd_free[hc % 2])
        trd = dma(WDRD[hc % 2], wd_hbm[:, hc * 8192:hc * 8192 + 8192], waits=w, slot=f"wdrd{hc % 2}")
        mm_last = [None, None]
        for ib in range(64):
            for th in range(2):
                if ib == 0:
                    wmm = [trd, t_inter]
                    if dev_free[th] is not None:
                        wmm.append(dev_free[th])
                else:
                    wmm = []
                mm_last[th] = te(lambda h, hc=hc, ib=ib, th=th:
                                 h.matmul(ps_f[2 * (hc % 2) + th][:],
                                          WDRD[hc % 2][:, ib * 128:ib * 128 + 128],
                                          INTER_SB[:, ib * 1024 + 512 * th:ib * 1024 + 512 * th + 512],
                                          start=(ib == 0), stop=(ib == 63)), waits=wmm)
        wd_free[hc % 2] = mm_last[1]
        for th in range(2):
            ev = ac(lambda h, hc=hc, th=th:
                    h.activation(DEV[th], ps_f[2 * (hc % 2) + th][:], AF.Copy),
                    waits=[mm_last[th]])
            od = dma_out(ffn_hbm[128 * hc:128 * hc + 128, 512 * th:512 * th + 512],
                         DEV[th], waits=[ev], slot=f"dev{th}")
            dev_free[th] = od

    barrier()

    prev_end = None
    for qt in range(8):
        w = [prev_end] if prev_end else []
        tf = p.emit(E_SY, lambda h, qt=qt: h.dma_start(
            out=FT8, in_=ffn_hbm[:, 128 * qt:128 * qt + 128], transpose=True),
            inc=(_slot("ft8"), 16), waits=w)
        tx = dma(XB[0], x1_hbm[128 * qt:128 * qt + 128, :], waits=w, slot="xb0")
        s10 = ln_stats(FT8, [tf] + w)
        a1 = dv(lambda h: h.tensor_scalar(X8A, FT8, sc_rstd, sc_nb,
                                          ALU.mult, ALU.add), waits=[s10])
        a2 = dv(lambda h: h.tensor_add(X8B, X8A, XB[0]), waits=[a1, tx])
        prev_end = a2
        dma_out(out_h[128 * qt:128 * qt + 128, :], X8B, waits=[a2], slot="outh")

    barrier()

    with es:
        with nc.Block() as block:
            @block.sync
            def _(h):
                p.replay(E_SY, h)

            @block.tensor
            def _(h):
                p.replay(E_TE, h)

            @block.vector
            def _(h):
                p.replay(E_DV, h)

            @block.scalar
            def _(h):
                p.replay(E_AC, h)
    return nc


_NC_CACHE = {}


def kernel(**inputs):
    if "nc" not in _NC_CACHE:
        _NC_CACHE["nc"] = build_program()
    nc = _NC_CACHE["nc"]
    in_maps = [_core_inputs(inputs, cid // 4, cid % 4) for cid in range(8)]
    res = run_bass_kernel_spmd(nc, in_maps, core_ids=list(range(8)))
    out = np.zeros((B, N, D), dtype=np.float32)
    for cid in range(8):
        b, c = cid // 4, cid % 4
        o = np.asarray(res.results[cid]["out"])
        rows = ((np.arange(8) * 512)[:, None] + 128 * c
                + np.arange(128)[None, :]).reshape(-1)
        out[b, rows] = o
    return out


# version 04
# speedup vs baseline: 1.0257x; 1.0257x over previous
import os
import subprocess
import sys
import tempfile
from contextlib import ExitStack
from math import comb

import numpy as np
import ml_dtypes

import concourse.bass as bass
import concourse.mybir as mybir
from concourse.bass_utils import run_bass_kernel_spmd

F32 = mybir.dt.float32
BF16 = mybir.dt.bfloat16
U8 = mybir.dt.uint8
AF = mybir.ActivationFunctionType
ALU = mybir.AluOpType
AX = mybir.AxisListType
BF = ml_dtypes.bfloat16

B, N, D = 2, 4096, 2048
H, KVH, HD = 16, 4, 128
INTER = 8192
N_CTRL = 130156
NSAMP = INTER * D
NCOL = NSAMP // 128
NSUB = NCOL // 512
NTILE = (NSUB + 2) // 3
OWN = 1024
EXP_SHIFT = -12.0
VW = 136

_CONST = {}
DEBUG = False


def _rope_tables():
    if "rope" in _CONST:
        return _CONST["rope"]
    prog = r"""
import jax
jax.config.update('jax_platforms', 'cpu')
jax.config.update('jax_enable_x64', True)
import jax.numpy as jnp, numpy as np, sys
HD, N, base = 128, 4096, 10000.0
inv = 1.0 / base ** (jnp.arange(0, HD, 2, dtype=jnp.float32) / HD)
t = jnp.arange(N, dtype=jnp.float32)
fr = jnp.outer(t, inv)
emb = jnp.concatenate([fr, fr], -1)
np.asarray(jnp.cos(emb)).astype(np.float32).tofile(sys.argv[1] + ".cos")
np.asarray(jnp.sin(emb)).astype(np.float32).tofile(sys.argv[1] + ".sin")
"""
    with tempfile.TemporaryDirectory() as td:
        pth = os.path.join(td, "rope.npz")
        env = dict(os.environ)
        env.pop("JAX_PLATFORMS", None)
        r = subprocess.run([sys.executable, "-c", prog, pth], env=env,
                           capture_output=True)
        if r.returncode != 0:
            raise RuntimeError(f"rope subprocess failed: {r.stderr[-2000:]}")
        cos = np.fromfile(pth + ".cos", dtype=np.float32).reshape(N, HD)
        sin = np.fromfile(pth + ".sin", dtype=np.float32).reshape(N, HD)
    ss = sin.copy()
    ss[:, :64] *= -1.0
    _CONST["rope"] = (cos, ss)
    return _CONST["rope"]


def _spline_packed():
    if "packed" in _CONST:
        return _CONST["packed"]
    t = np.linspace(0.0, N_CTRL - 1.0, NSAMP, dtype=np.float64)
    i = np.clip(np.floor(t).astype(np.int64), 0, N_CTRL - 2)
    i2 = i.reshape(NCOL, 128)
    t2 = t.reshape(NCOL, 128)
    i_c = i2[:, 0]
    b_c = (i2 == i_c[:, None]).sum(axis=1).astype(np.int64)
    phi = t2[:, 0] - i_c
    delta = (N_CTRL - 1.0) / (NSAMP - 1.0)
    tap_idx = np.clip(i_c[:, None] + np.arange(-1, 4)[None, :], 0, N_CTRL - 1)
    beta = np.array([[0.0, -0.5, 1.0, -0.5],
                     [1.0, 0.0, -2.5, 1.5],
                     [0.0, 0.5, 2.0, -1.5],
                     [0.0, 0.0, -0.5, 0.5]])
    pr = np.arange(128, dtype=np.float64)
    G = np.zeros((4, 4, 128))
    for k in range(4):
        for m in range(4):
            for q in range(m, 4):
                G[m, k] += beta[k, q] * comb(q, m) * (pr * delta) ** (q - m)
    G = G.reshape(16, 128)
    u_pow = phi[:, None] ** np.arange(4)[None, :]
    v_pow = (phi[:, None] - 1.0) ** np.arange(4)[None, :]
    U = np.zeros((NTILE, 128, 512), dtype=np.float64)
    TIDX = np.zeros((NSUB, 32, 512), dtype=np.int64)
    M = np.zeros((NSUB, 128, 512), dtype=np.uint8)
    pcol = np.arange(128)
    for gs in range(NSUB):
        ti, sl = gs // 3, gs % 3
        cols = np.arange(512 * gs, 512 * gs + 512)
        for m in range(4):
            for k in range(4):
                U[ti, 32 * sl + m * 4 + k, :] = u_pow[cols, m]
                U[ti, 32 * sl + 16 + m * 4 + k, :] = v_pow[cols, m]
                TIDX[gs, m * 4 + k, :] = tap_idx[cols, k]
                TIDX[gs, 16 + m * 4 + k, :] = tap_idx[cols, k + 1]
        M[gs] = (pcol[:, None] < b_c[cols][None, :]).astype(np.uint8)
    GST = np.zeros((128, 256), dtype=np.float64)
    for sl in range(3):
        GST[32 * sl:32 * sl + 16, 0:128] = G
        GST[32 * sl + 16:32 * sl + 32, 128:256] = G
    _CONST["packed"] = (U.astype(BF).reshape(NTILE * 128, 512),
                        M.reshape(NSUB * 128, 512), GST.astype(BF), TIDX)
    return _CONST["packed"]


def _core_inputs(inputs, b, c):
    U, M, GST, TIDX = _spline_packed()
    cos, ss = _rope_tables()
    x = np.asarray(inputs["L0_attn_x"][b])
    order = [(c + s) % 4 for s in range(4)]
    xp = np.ascontiguousarray(x.reshape(8, 4, 128, D)[:, order].reshape(N, D))
    cosp = cos.reshape(8, 4, 128, HD)[:, order].reshape(N, HD)
    ssp = ss.reshape(8, 4, 128, HD)[:, order].reshape(N, HD)
    coskr = np.ascontiguousarray(np.tile(cosp, (1, KVH)))
    sskr = np.ascontiguousarray(np.tile(ssp, (1, KVH)))
    own_rows = ((np.arange(8) * 512)[:, None] + 128 * c
                + np.arange(128)[None, :]).reshape(-1)
    s = np.float64(HD) ** -0.5
    cosqr = np.ascontiguousarray(np.tile((cos[own_rows].astype(np.float64) * s).astype(np.float32), (1, H)))
    ssqr = np.ascontiguousarray(np.tile((ss[own_rows].astype(np.float64) * s).astype(np.float32), (1, H)))
    tri = np.zeros((128, 512), dtype=np.float32)
    kk = np.arange(128)[:, None]
    pq = np.arange(128)[None, :]
    tri[:, 0:128] = np.where(kk <= pq, 0.0, -1e9)
    for s in range(1, 4):
        tri[:, 128 * s:128 * s + 128] = 0.0 if (c + s) % 4 < c else -1e9
    taps = {}
    for nm, key in (("tg", "L0_ffn_gate_cp"), ("tu", "L0_ffn_up_cp"),
                    ("td", "L0_ffn_down_cp")):
        cp = np.asarray(inputs[key])
        T = np.zeros((NTILE, 128, 512), dtype=BF)
        for gs in range(NSUB):
            T[gs // 3, 32 * (gs % 3):32 * (gs % 3) + 32, :] = cp[TIDX[gs]].astype(BF)
        taps[nm] = T.reshape(NTILE * 128, 512)
    return {
        "xp": xp, "coskr": coskr, "sskr": sskr, "cosqr": cosqr, "ssqr": ssqr,
        "tri_t": tri, "ident": np.eye(128, dtype=BF), "gst": GST,
        "uarr": U, "marr": M, "tg": taps["tg"], "tu": taps["tu"], "td": taps["td"],
        "wq": np.ascontiguousarray(inputs["L0_attn_Wq"]),
        "wk": np.ascontiguousarray(inputs["L0_attn_Wk"]),
        "wv": np.ascontiguousarray(inputs["L0_attn_Wv"]),
        "wo": np.ascontiguousarray(inputs["L0_attn_Wo"]),
    }


class Prog:
    def __init__(self):
        self.items = []
        self.cnt = {}
        self.sems = {}
        self.last_wait = {}

    def emit(self, eng, fn, inc=None, waits=()):
        for tok in waits:
            self._wait(eng, tok)
        tok_out = None
        if inc is not None:
            sem, amt = inc
            self.cnt[sem] = self.cnt.get(sem, 0) + amt
            tok_out = (sem, self.cnt[sem])

            def wrapped(h, fn=fn, sem=sem, amt=amt):
                fn(h).then_inc(self.sems[sem], amt)
        else:
            def wrapped(h, fn=fn):
                fn(h)
        self.items.append((eng, wrapped))
        return tok_out

    def _wait(self, eng, tok):
        if tok is None:
            return
        sem, val = tok
        if self.last_wait.get((eng, sem), -1) >= val:
            return
        self.last_wait[(eng, sem)] = val

        def w(h, sem=sem, val=val):
            h.wait_ge(self.sems[sem], val)
        self.items.append((eng, w))

    def all_toks(self):
        return [(s, v) for s, v in self.cnt.items()]

    def replay(self, eng, handle):
        for e, fn in self.items:
            if e == eng:
                fn(handle)


def build_program():
    nc = bass.Bass("TRN2", target_bir_lowering=False)
    d = {}

    def din(name, shape, dt):
        d[name] = nc.dram_tensor(name, shape, dt, kind="ExternalInput").ap()

    din("xp", [N, D], F32)
    din("coskr", [N, 512], F32)
    din("sskr", [N, 512], F32)
    din("cosqr", [OWN, D], F32)
    din("ssqr", [OWN, D], F32)
    din("tri_t", [128, 512], F32)
    din("ident", [128, 128], BF16)
    din("gst", [128, 256], BF16)
    din("uarr", [NTILE * 128, 512], BF16)
    din("marr", [NSUB * 128, 512], U8)
    din("tg", [NTILE * 128, 512], BF16)
    din("tu", [NTILE * 128, 512], BF16)
    din("td", [NTILE * 128, 512], BF16)
    din("wq", [D, D], F32)
    din("wk", [D, 512], F32)
    din("wv", [D, 512], F32)
    din("wo", [D, D], F32)
    out_h = nc.dram_tensor("out", [OWN, D], F32, kind="ExternalOutput").ap()
    dbg = {}
    if DEBUG:
        for nm, shp, dt in (("dbg_kfm", [128, KVH * N], BF16),
                            ("dbg_vtm", [128, KVH * 32 * VW], BF16),
                            ("dbg_qfm", [128, H * OWN], BF16),
                            ("dbg_atm", [128, 8 * D], BF16),
                            ("dbg_ln2fm", [128, 16 * OWN], BF16),
                            ("dbg_inter", [128, 65536], BF16),
                            ("dbg_fmb", [128, 8192], BF16)):
            dbg[nm] = nc.dram_tensor(nm, shp, dt, kind="ExternalOutput").ap()
    _scr_kind = "ExternalOutput" if DEBUG else "Internal"
    wd_hbm = nc.dram_tensor("wd_hbm", [128, NCOL], BF16, kind=_scr_kind).ap()
    o_hbm = nc.dram_tensor("o_hbm", [OWN, D], F32, kind=_scr_kind).ap()
    x1_hbm = nc.dram_tensor("x1_hbm", [OWN, D], F32, kind=_scr_kind).ap()
    ffn_hbm = nc.dram_tensor("ffn_hbm", [D, OWN], BF16, kind=_scr_kind).ap()

    es = ExitStack()
    ABF = es.enter_context(nc.sbuf_tensor("abf", [128, 96256], BF16)).ap()
    AF32 = es.enter_context(nc.sbuf_tensor("af32", [128, 4736], F32)).ap()
    AU8 = es.enter_context(nc.sbuf_tensor("au8", [128, 1024], U8)).ap()
    ps_f = [es.enter_context(nc.psum_tensor(f"psf{i}", [128, 512], F32)).ap()
            for i in range(6)]
    ps_b = [es.enter_context(nc.psum_tensor(f"psb{i}", [128, 1024], BF16)).ap()
            for i in range(2)]

    KFM = ABF[:, 0:16384]
    VTM = ABF[:, 16384:33792]
    WKV = ABF[:, 33792:50176]
    QFM = ABF[:, 33792:50176]
    AFM = ABF[:, 33792:50176]
    FMB = ABF[:, 50176:58368]
    WQ4 = ABF[:, 50176:58368]
    LNT = ABF[:, 58368:60416]
    WOH = ABF[:, 50176:66560]
    OWNFM = ABF[:, 66560:82944]
    ATM = ABF[:, 66560:82944]
    LN2FM = ABF[:, 66560:82944]
    WDRD = [ABF[:, 66560:74752], ABF[:, 74752:82944]]
    DSTG = ABF[:, 0:8192]
    INTER_SB = ABF[:, 0:65536]
    CKT = [ABF[:, 82944 + i * 1024:82944 + (i + 1) * 1024].bitcast(F32) for i in range(2)]
    SKT = [ABF[:, 84992 + i * 1024:84992 + (i + 1) * 1024].bitcast(F32) for i in range(2)]
    KSW = ABF[:, 87040:88064].bitcast(F32)
    KRP = ABF[:, 88064:89088].bitcast(F32)
    KPB = ABF[:, 89088:89600]
    PTB = [ABF[:, 82944:83456], ABF[:, 83456:83968]]
    TT = [ABF[:, 82944 + i * 512:82944 + (i + 1) * 512] for i in range(2)]
    UT = [ABF[:, 83968 + i * 512:83968 + (i + 1) * 512] for i in range(2)]
    F12T = [ABF[:, 84992 + i * 512:84992 + (i + 1) * 512] for i in range(2)]
    WGB = [ABF[:, 86016 + i * 2048:86016 + (i + 1) * 2048] for i in range(2)]
    WUB = [ABF[:, 90112 + i * 2048:90112 + (i + 1) * 2048] for i in range(2)]
    DEV = [ABF[:, 94208 + i * 512:94208 + (i + 1) * 512] for i in range(2)]
    SIL = ABF[:, 95616:96128]
    X1T = ABF[:, 8192:12288].bitcast(F32)
    X1T2 = ABF[:, 12288:16384].bitcast(F32)
    EV4 = [X1T, X1T2]
    FT8 = ABF[:, 16384:18432]
    X8A = ABF[:, 18432:22528].bitcast(F32)
    X8B = ABF[:, 22528:26624].bitcast(F32)
    GSTS = ABF[:, 95232:95488]
    IDENT = ABF[:, 95488:95616]
    XB = [AF32[:, 0:2048], AF32[:, 2048:4096]]
    TRI = AF32[:, 4096:4608]
    STATS = AF32[:, 4608:4736]
    sc_sum = STATS[:, 0:1]
    sc_ssq = STATS[:, 1:2]
    sc_mean = STATS[:, 2:3]
    sc_var = STATS[:, 3:4]
    sc_r1 = STATS[:, 4:5]
    sc_rstd = STATS[:, 5:6]
    sc_nb = STATS[:, 6:7]
    sc_t = STATS[:, 7:8]
    sc_rs = [STATS[:, 8 + i:9 + i] for i in range(2)]
    sc_nm5 = STATS[:, 10:11]
    sc_shift = STATS[:, 11:12]
    MSK = [AU8[:, 0:512], AU8[:, 512:1024]]

    p = Prog()
    for s in ("te", "dv", "ac"):
        p.sems[s] = es.enter_context(nc.semaphore(s))
    _slot_sems = {}

    def _slot(name):
        if name not in _slot_sems:
            p.sems["s_" + name] = es.enter_context(nc.semaphore("s_" + name))
            _slot_sems[name] = "s_" + name
        return _slot_sems[name]

    E_SY, E_TE, E_DV, E_AC = "sync", "tensor", "vector", "scalar"

    def dma(dst, src, waits=(), slot="misc"):
        return p.emit(E_SY, lambda h, a=dst, b=src: h.dma_start(out=a, in_=b),
                      inc=(_slot(name=slot), 16), waits=waits)

    def dma_out(dst, src, waits=(), slot="out"):
        return p.emit(E_SY, lambda h, a=dst, b=src: h.dma_start(out=a, in_=b),
                      inc=(_slot(name=slot), 16), waits=waits)

    def te(fn, waits=()):
        return p.emit(E_TE, fn, inc=("te", 1), waits=waits)

    def dv(fn, waits=()):
        return p.emit(E_DV, fn, inc=("dv", 1), waits=waits)

    def ac(fn, waits=()):
        return p.emit(E_AC, fn, inc=("ac", 1), waits=waits)

    def barrier():
        toks = p.all_toks()
        for eng in (E_SY, E_TE, E_DV, E_AC):
            for t in toks:
                p._wait(eng, t)

    psb_free = [None, None]

    def transpose_evac(src_tile, dst_tile, bank, waits):
        w = list(waits) + [t_id]
        if psb_free[bank] is not None:
            w.append(psb_free[bank])
        tr = te(lambda h, b=bank, s=src_tile:
                h.transpose(ps_b[b][:, 0:128], s, IDENT), waits=w)
        ev = dv(lambda h, b=bank, o=dst_tile: h.tensor_copy(o, ps_b[b][:, 0:128]),
                waits=[tr])
        psb_free[bank] = ev
        return tr, ev

    t_id = dma(IDENT, d["ident"][:], slot="ident")
    t_gst = dma(GSTS, d["gst"][:], slot="gst")
    t_tri = dma(TRI, d["tri_t"][:], slot="tri")
    t_shift = dv(lambda h: h.memset(sc_shift, EXP_SHIFT))
    vt_v = VTM.rearrange("p (t w) -> p t w", w=VW)
    z0 = dv(lambda h: h.memset(vt_v[:, :, 128:VW], 0.0))
    t_ones = dv(lambda h: h.memset(vt_v[:, :, 128:129], 1.0), waits=[z0])

    prev = None
    for dc in range(16):
        w = [prev] if prev else []
        tl = dma(XB[dc % 2][:, 0:512], d["wk"][128 * dc:128 * dc + 128, :], waits=w, slot=f"xb{dc % 2}")
        dv(lambda h, dc=dc: h.tensor_copy(WKV[:, dc * 512:dc * 512 + 512],
                                          XB[dc % 2][:, 0:512]), waits=[tl])
        tl2 = dma(XB[dc % 2][:, 512:1024], d["wv"][128 * dc:128 * dc + 128, :], waits=[tl], slot=f"xb{dc % 2}")
        prev = dv(lambda h, dc=dc: h.tensor_copy(
            WKV[:, 8192 + dc * 512:8192 + dc * 512 + 512],
            XB[dc % 2][:, 512:1024]), waits=[tl2])
    t_wkv = prev

    def ln_tile(xt, src_rows, prev_apply=None, lnt_guard=None, extra=()):
        tl = dma(xt, src_rows, waits=[x for x in (prev_apply,) if x] + list(extra))
        w0 = [tl] + [x for x in (prev_apply, lnt_guard) if x]
        s1 = dv(lambda h: h.reduce_sum(sc_sum, xt, axis=AX.X), waits=w0)
        s2 = ac(lambda h: h.activation(LNT, xt, AF.Square, accum_out=sc_ssq), waits=w0)
        s3 = dv(lambda h: h.tensor_scalar_mul(sc_mean, sc_sum, 1.0 / D), waits=[s1])
        s4 = dv(lambda h: h.tensor_scalar_mul(sc_var, sc_ssq, 1.0 / D), waits=[s2])
        s5 = dv(lambda h: h.tensor_mul(sc_t, sc_mean, sc_mean), waits=[s3])
        s6 = dv(lambda h: h.tensor_sub(sc_var, sc_var, sc_t), waits=[s4, s5])
        s7 = dv(lambda h: h.tensor_scalar_add(sc_var, sc_var, 1e-5), waits=[s6])
        s8 = dv(lambda h: h.reciprocal(sc_r1, sc_var), waits=[s7])
        s9 = ac(lambda h: h.activation(sc_rstd, sc_r1, AF.Sqrt), waits=[s8])
        s10 = dv(lambda h: h.tensor_scalar_mul(sc_nb, sc_mean, -1.0), waits=[s9])
        s11 = dv(lambda h: h.tensor_scalar(LNT, xt, sc_nb, sc_rstd,
                                           ALU.add, ALU.mult), waits=[s10])
        return tl, s11

    prev_apply = None
    last_tr = None
    fm_consumed = None
    kfm_done = []
    vtm_done = []
    ckt_free = [None, None]
    skt_free = [None, None]
    kpb_free = None
    for tt in range(32):
        pc, sub = tt // 4, tt % 4
        xt = XB[tt % 2]
        tl, app = ln_tile(xt, d["xp"][128 * tt:128 * tt + 128, :],
                          prev_apply=prev_apply, lnt_guard=last_tr,
                          extra=([t_wkv] if tt < 2 else ()))
        prev_apply = app
        evs = []
        for dc in range(16):
            w = [app]
            if fm_consumed is not None:
                w.append(fm_consumed)
            tr, ev = transpose_evac(
                LNT[:, dc * 128:dc * 128 + 128],
                FMB[:, dc * 512 + 128 * sub:dc * 512 + 128 * sub + 128],
                dc % 2, w)
            evs.append(ev)
            last_tr = tr
        if sub < 3:
            continue
        for st in range(4):
            gt = 4 * pc + st
            pk = ps_f[2 * (st % 2)]
            pv = ps_f[2 * (st % 2) + 1]
            mm_last = None
            for dc in range(16):
                if dc == 0:
                    w = [t_wkv, evs[-1]]
                    if kfm_done:
                        w.append(kfm_done[-1])
                    if vtm_done:
                        w.append(vtm_done[-1])
                else:
                    w = []
                te(lambda h, pk=pk, dc=dc, st=st:
                   h.matmul(pk[:], FMB[:, dc * 512 + 128 * st:dc * 512 + 128 * st + 128],
                            WKV[:, dc * 512:dc * 512 + 512],
                            start=(dc == 0), stop=(dc == 15)), waits=w)
                mm_last = te(lambda h, pv=pv, dc=dc, st=st:
                             h.matmul(pv[:], FMB[:, dc * 512 + 128 * st:dc * 512 + 128 * st + 128],
                                      WKV[:, 8192 + dc * 512:8192 + dc * 512 + 512],
                                      start=(dc == 0), stop=(dc == 15)), waits=[])
            ck = dma(CKT[st % 2], d["coskr"][128 * gt:128 * gt + 128, :],
                     waits=[ckt_free[st % 2]] if ckt_free[st % 2] else [], slot=f"ck{st % 2}")
            sk = dma(SKT[st % 2], d["sskr"][128 * gt:128 * gt + 128, :],
                     waits=[skt_free[st % 2]] if skt_free[st % 2] else [], slot=f"sk{st % 2}")
            kv3 = lambda ap: ap.rearrange("p (g h w) -> p (g h) w", h=2, w=64)
            r1 = dv(lambda h, pk=pk: h.tensor_copy(
                kv3(KSW)[:, 0::2, :], kv3(pk[:])[:, 1::2, :]), waits=[mm_last])
            r2 = dv(lambda h, pk=pk: h.tensor_copy(
                kv3(KSW)[:, 1::2, :], kv3(pk[:])[:, 0::2, :]), waits=[r1])
            r3 = dv(lambda h, st=st: h.tensor_mul(KRP, KSW, SKT[st % 2]),
                    waits=[r2, sk])
            r4 = dv(lambda h, pk=pk, st=st: h.tensor_mul(KSW, pk[:], CKT[st % 2]),
                    waits=[ck, r3])
            skt_free[st % 2] = r3
            ckt_free[st % 2] = r4
            w5 = [r4]
            if kpb_free is not None:
                w5.append(kpb_free)
            r5 = dv(lambda h: h.tensor_add(KPB, KSW, KRP), waits=w5)
            for g in range(KVH):
                trk, evk = transpose_evac(
                    KPB[:, g * 128:g * 128 + 128],
                    KFM[:, g * 4096 + 128 * gt:g * 4096 + 128 * gt + 128],
                    g % 2, [r5])
                kfm_done.append(evk)
                kpb_free = trk
                evv = ac(lambda h, pv=pv, g=g, gt=gt:
                         h.activation(VTM[:, g * 32 * VW + gt * VW:g * 32 * VW + gt * VW + 128],
                                      pv[:, g * 128:g * 128 + 128], AF.Copy),
                         waits=[mm_last, t_ones])
                vtm_done.append(evv)
        fm_consumed = kfm_done[-1]

    barrier()
    if DEBUG:
        dma_out(dbg["dbg_kfm"], KFM)
        dma_out(dbg["dbg_vtm"], VTM)
        dma_out(dbg["dbg_fmb"], FMB)
        barrier()

    prev_apply = None
    last_tr = None
    own_evs = []
    for tt in range(8):
        xt = XB[tt % 2]
        tl, app = ln_tile(xt, d["xp"][512 * tt:512 * tt + 128, :],
                          prev_apply=prev_apply, lnt_guard=last_tr)
        prev_apply = app
        for dc in range(16):
            tr, ev = transpose_evac(
                LNT[:, dc * 128:dc * 128 + 128],
                OWNFM[:, dc * 1024 + tt * 128:dc * 1024 + tt * 128 + 128],
                dc % 2, [app])
            own_evs.append(ev)
            last_tr = tr
    t_own = own_evs[-1]

    qfm_done = []
    wq_c = []
    kpb_free = None
    for hq in range(4):
        for dc in range(16):
            w = []
            if len(wq_c) >= 2:
                w.append(wq_c[-2])
            elif prev_apply is not None:
                w.append(prev_apply)
            if hq > 0:
                w.append(("te", p.cnt["te"]))
            tl = dma(XB[dc % 2][:, 0:512],
                     d["wq"][128 * dc:128 * dc + 128, 512 * hq:512 * hq + 512],
                     waits=w, slot=f"xb{dc % 2}")
            cc = dv(lambda h, dc=dc: h.tensor_copy(WQ4[:, dc * 512:dc * 512 + 512],
                                                   XB[dc % 2][:, 0:512]), waits=[tl])
            wq_c.append(cc)
        for tt in range(8):
            pq = ps_f[tt % 2]
            mm_last = None
            for dc in range(16):
                if dc == 0:
                    w = [wq_c[-1], t_own]
                    if len(qfm_done) >= 8:
                        w.append(qfm_done[-8])
                else:
                    w = []
                mm_last = te(lambda h, pq=pq, dc=dc, tt=tt:
                             h.matmul(pq[:], OWNFM[:, dc * 1024 + tt * 128:dc * 1024 + tt * 128 + 128],
                                      WQ4[:, dc * 512:dc * 512 + 512],
                                      start=(dc == 0), stop=(dc == 15)), waits=w)
            ck = dma(CKT[tt % 2], d["cosqr"][128 * tt:128 * tt + 128, 512 * hq:512 * hq + 512],
                     waits=[ckt_free[tt % 2]] if ckt_free[tt % 2] else [], slot=f"ck{tt % 2}")
            sk = dma(SKT[tt % 2], d["ssqr"][128 * tt:128 * tt + 128, 512 * hq:512 * hq + 512],
                     waits=[skt_free[tt % 2]] if skt_free[tt % 2] else [], slot=f"sk{tt % 2}")
            kv3 = lambda ap: ap.rearrange("p (g h w) -> p (g h) w", h=2, w=64)
            r1 = dv(lambda h, pq=pq: h.tensor_copy(
                kv3(KSW)[:, 0::2, :], kv3(pq[:])[:, 1::2, :]), waits=[mm_last])
            r2 = dv(lambda h, pq=pq: h.tensor_copy(
                kv3(KSW)[:, 1::2, :], kv3(pq[:])[:, 0::2, :]), waits=[r1])
            r3 = dv(lambda h, tt=tt: h.tensor_mul(KRP, KSW, SKT[tt % 2]),
                    waits=[r2, sk])
            r4 = dv(lambda h, pq=pq, tt=tt: h.tensor_mul(KSW, pq[:], CKT[tt % 2]),
                    waits=[ck, r3])
            skt_free[tt % 2] = r3
            ckt_free[tt % 2] = r4
            w5 = [r4]
            if kpb_free is not None:
                w5.append(kpb_free)
            r5 = dv(lambda h: h.tensor_add(KPB, KSW, KRP), waits=w5)
            for gg in range(4):
                hh = 4 * hq + gg
                trq, evq = transpose_evac(
                    KPB[:, gg * 128:gg * 128 + 128],
                    QFM[:, hh * 1024 + tt * 128:hh * 1024 + tt * 128 + 128],
                    gg % 2, [r5])
                qfm_done.append(evq)
                kpb_free = trq

    barrier()
    if DEBUG:
        dma_out(dbg["dbg_qfm"], QFM)
        barrier()

    def load_wo_half(dh, gate_tok):
        prev_c = gate_tok
        for acx in range(16):
            t1 = dma(XB[acx % 2][:, 0:1024],
                     d["wo"][128 * acx:128 * acx + 128, 1024 * dh:1024 * dh + 1024],
                     waits=[prev_c] if prev_c else [], slot=f"xb{acx % 2}")
            prev_c = dv(lambda h, acx=acx: h.tensor_copy(
                WOH[:, acx * 1024:acx * 1024 + 1024], XB[acx % 2][:, 0:1024]),
                waits=[t1])
        return prev_c

    t_woh = load_wo_half(0, None)

    atm_done = []
    pt_free = [None, None]
    st_free = [None, None]
    av_free = [None, None]
    rs_free = [None, None]
    t_q = qfm_done[-1]
    t_k = kfm_done[-1]
    t_v = vtm_done[-1]
    for hh in range(H):
        g = hh // 4
        for qt in range(8):
            avb = ps_f[2 + (qt % 2)]
            av_last = None
            for kb in range(qt + 1):
                sbk = ps_f[4 + (kb % 2)]
                mms = []
                for ci in range(4):
                    if ci == 0:
                        w = [t_q, t_k]
                        if st_free[kb % 2] is not None:
                            w.append(st_free[kb % 2])
                    else:
                        w = []
                    mms.append(te(lambda h, sbk=sbk, g=g, kb=kb, ci=ci, hh=hh, qt=qt:
                                  h.matmul(sbk[:, 128 * ci:128 * ci + 128],
                                           KFM[:, g * 4096 + 512 * kb + 128 * ci:
                                               g * 4096 + 512 * kb + 128 * ci + 128],
                                           QFM[:, hh * 1024 + qt * 128:hh * 1024 + qt * 128 + 128],
                                           start=True, stop=True), waits=w))
                if kb == qt:
                    mk = dv(lambda h, sbk=sbk: h.tensor_add(sbk[:], sbk[:], TRI),
                            waits=mms + [t_tri])
                    exp_w = [mk]
                else:
                    exp_w = [mms[-1]]
                if pt_free[kb % 2] is not None:
                    exp_w = exp_w + [pt_free[kb % 2]]
                ex = ac(lambda h, sbk=sbk, kb=kb:
                        h.activation(PTB[kb % 2], sbk[:], AF.Exp, bias=sc_shift),
                        waits=exp_w + [t_shift])
                st_free[kb % 2] = ex
                for ci in range(4):
                    w = [ex]
                    if kb == 0 and ci == 0:
                        w.append(t_v)
                        if av_free[qt % 2] is not None:
                            w.append(av_free[qt % 2])
                    av_last = te(lambda h, avb=avb, g=g, kb=kb, ci=ci:
                                 h.matmul(avb[:, 0:VW],
                                          PTB[kb % 2][:, 128 * ci:128 * ci + 128],
                                          VTM[:, g * 32 * VW + (4 * kb + ci) * VW:
                                              g * 32 * VW + (4 * kb + ci) * VW + VW],
                                          start=(kb == 0 and ci == 0),
                                          stop=(kb == qt and ci == 3)), waits=w)
                pt_free[kb % 2] = av_last
            rsl = sc_rs[qt % 2]
            w = [av_last]
            if rs_free[qt % 2] is not None:
                w.append(rs_free[qt % 2])
            rc = dv(lambda h, avb=avb, rsl=rsl: h.reciprocal(rsl, avb[:, 128:129]),
                    waits=w)
            evo = dv(lambda h, avb=avb, rsl=rsl, hh=hh, qt=qt:
                     h.tensor_scalar_mul(ATM[:, qt * 2048 + hh * 128:qt * 2048 + hh * 128 + 128],
                                         avb[:, 0:128], rsl), waits=[rc])
            rs_free[qt % 2] = evo
            av_free[qt % 2] = evo
            atm_done.append(evo)

    if DEBUG:
        dma_out(dbg["dbg_atm"], ATM, waits=[atm_done[-1]])
    barrier()

    def recon_sub(sl, fslot, f12_tok, sel_view, sel_waits, psum_guard, msk):
        w1w = [f12_tok, t_gst]
        if psum_guard is not None:
            w1w.append(psum_guard)
        w1 = te(lambda h, sl=sl, fslot=fslot:
                h.matmul(ps_f[4][:], GSTS[32 * sl:32 * sl + 32, 0:128],
                         F12T[fslot][32 * sl:32 * sl + 32, :],
                         start=True, stop=True, tile_position=(32 * sl, 0)),
                waits=w1w)
        te(lambda h, sl=sl, fslot=fslot:
           h.matmul(ps_f[5][:], GSTS[32 * sl:32 * sl + 32, 128:256],
                    F12T[fslot][32 * sl:32 * sl + 32, :],
                    start=True, stop=True, tile_position=(32 * sl, 0)), waits=[])
        t_w2 = ("te", p.cnt["te"])
        sel = dv(lambda h, v=sel_view, m=msk:
                 h.select(v, m, ps_f[4][:], ps_f[5][:]),
                 waits=[t_w2] + list(sel_waits))
        return sel

    down_sel = None
    stg_dma = None
    f12_tok = None
    for gs in range(NSUB):
        hc = gs // 16
        hl = 8 * (gs % 16)
        ti, parf = gs // 3, (gs // 3) % 2
        if gs % 3 == 0:
            wprev = [down_sel] if down_sel else []
            ut_l = dma(UT[parf], d["uarr"][128 * ti:128 * ti + 128, :], waits=wprev, slot=f"ut{parf}")
            tt_l = dma(TT[parf], d["td"][128 * ti:128 * ti + 128, :], waits=wprev, slot=f"tt{parf}")
            f12_tok = dv(lambda h, parf=parf: h.tensor_mul(F12T[parf], UT[parf], TT[parf]),
                         waits=[ut_l, tt_l])
        mtok = dma(MSK[gs % 2], d["marr"][128 * gs:128 * gs + 128, :],
                   waits=[down_sel] if down_sel else [], slot=f"msk{gs % 2}")
        sel_w = [mtok]
        if stg_dma is not None and gs % 16 == 0:
            sel_w.append(stg_dma)
        view = DSTG.rearrange("p (ic hl) -> p hl ic", hl=128)[:, hl:hl + 8, :]
        down_sel = recon_sub(gs % 3, parf, f12_tok, view, sel_w, down_sel, MSK[gs % 2])
        if gs % 16 == 15:
            stg_dma = dma_out(wd_hbm[:, hc * 8192:hc * 8192 + 8192], DSTG,
                              waits=[down_sel], slot="stg")
    t_wd_written = stg_dma

    afm_done = []
    t_atm = atm_done[-1]
    for qt in range(8):
        for acx in range(16):
            tr, ev = transpose_evac(
                ATM[:, qt * 2048 + acx * 128:qt * 2048 + acx * 128 + 128],
                AFM[:, (qt * 16 + acx) * 128:(qt * 16 + acx) * 128 + 128],
                acx % 2, [t_atm])
            afm_done.append(ev)
    t_afm = afm_done[-1]

    o_free = [None] * 4
    for dh in range(2):
        if dh == 1:
            t_woh = load_wo_half(1, ("te", p.cnt["te"]))
        for qt in range(8):
            pso = [ps_f[2 * (qt % 2)], ps_f[2 * (qt % 2) + 1]]
            mm_last = None
            for acx in range(16):
                for db in range(2):
                    if acx == 0 and db == 0:
                        w = [t_afm, t_woh]
                        for b in (0, 1):
                            if o_free[2 * (qt % 2) + b] is not None:
                                w.append(o_free[2 * (qt % 2) + b])
                    else:
                        w = []
                    mm_last = te(lambda h, pso=pso, qt=qt, acx=acx, db=db:
                                 h.matmul(pso[db][:],
                                          AFM[:, (qt * 16 + acx) * 128:(qt * 16 + acx) * 128 + 128],
                                          WOH[:, acx * 1024 + 512 * db:acx * 1024 + 512 * db + 512],
                                          start=(acx == 0), stop=(acx == 15)), waits=w)
            for db in range(2):
                evd = dv(lambda h, pso=pso, db=db, qt=qt:
                         h.tensor_copy(EV4[qt % 2][:, 512 * db:512 * db + 512], pso[db][:]),
                         waits=[mm_last])
                od = dma_out(o_hbm[128 * qt:128 * qt + 128,
                                   1024 * dh + 512 * db:1024 * dh + 512 * db + 512],
                             EV4[qt % 2][:, 512 * db:512 * db + 512], waits=[evd],
                             slot=f"oev{2 * (qt % 2) + db}")
                o_free[2 * (qt % 2) + db] = od

    barrier()

    def ln_stats(src, w0):
        s1 = dv(lambda h: h.reduce_sum(sc_sum, src, axis=AX.X), waits=w0)
        s2 = ac(lambda h: h.activation(LNT, src, AF.Square, accum_out=sc_ssq), waits=w0)
        s3 = dv(lambda h: h.tensor_scalar_mul(sc_mean, sc_sum, 1.0 / D), waits=[s1])
        s4 = dv(lambda h: h.tensor_scalar_mul(sc_var, sc_ssq, 1.0 / D), waits=[s2])
        s5 = dv(lambda h: h.tensor_mul(sc_t, sc_mean, sc_mean), waits=[s3])
        s6 = dv(lambda h: h.tensor_sub(sc_var, sc_var, sc_t), waits=[s4, s5])
        s7 = dv(lambda h: h.tensor_scalar_add(sc_var, sc_var, 1e-5), waits=[s6])
        s8 = dv(lambda h: h.reciprocal(sc_r1, sc_var), waits=[s7])
        s9 = ac(lambda h: h.activation(sc_rstd, sc_r1, AF.Sqrt), waits=[s8])
        s10a = dv(lambda h: h.scalar_tensor_tensor(sc_nb, sc_mean, -1.0, sc_rstd,
                                                   ALU.mult, ALU.mult), waits=[s9])
        s10 = dv(lambda h: h.tensor_scalar_mul(sc_nm5, sc_mean, -1.0), waits=[s10a])
        return s10

    prev_end = None
    last_tr = None
    ln2_done = []
    for qt in range(8):
        w = [prev_end] if prev_end else []
        to = dma(XB[0], o_hbm[128 * qt:128 * qt + 128, :], waits=w, slot="xb0")
        tx = dma(XB[1], d["xp"][512 * qt:512 * qt + 128, :], waits=w, slot="xb1")
        w0 = [to] + ([prev_end] if prev_end else []) + ([last_tr] if last_tr else [])
        s10 = ln_stats(XB[0], w0)
        a1 = dv(lambda h: h.tensor_scalar(X1T, XB[0], sc_rstd, sc_nb,
                                          ALU.mult, ALU.add), waits=[s10])
        a2 = dv(lambda h: h.tensor_add(X1T2, X1T, XB[1]), waits=[a1, tx])
        sp = dma_out(x1_hbm[128 * qt:128 * qt + 128, :], X1T2, waits=[a2], slot="x1sp")
        s10b = ln_stats(X1T2, [a2])
        appb = dv(lambda h: h.tensor_scalar(LNT, X1T2, sc_nm5, sc_rstd,
                                            ALU.add, ALU.mult), waits=[s10b, sp])
        prev_end = appb
        for dc in range(16):
            tr, ev = transpose_evac(
                LNT[:, dc * 128:dc * 128 + 128],
                LN2FM[:, dc * 1024 + qt * 128:dc * 1024 + qt * 128 + 128],
                dc % 2, [appb])
            ln2_done.append(ev)
            last_tr = tr
        prev_end = ev
    t_ln2 = ln2_done[-1]

    if DEBUG:
        dma_out(dbg["dbg_ln2fm"], LN2FM, waits=[t_ln2])
    barrier()

    gsel, usel = None, None
    f12g, f12u = None, None
    cur_tile = -1
    gu_free = [None, None]
    silu_toks = []
    inter_done = []
    sel_of = {}

    def recon_ib(ib):
        nonlocal gsel, usel, f12g, f12u, cur_tile
        par_b = ib % 2
        for s4 in range(4):
            gs = 4 * ib + s4
            ti, parf = gs // 3, (gs // 3) % 2
            if ti != cur_tile:
                cur_tile = ti
                wprev = [x for x in (gsel, usel) if x is not None]
                ut_l = dma(UT[parf], d["uarr"][128 * ti:128 * ti + 128, :], waits=wprev, slot=f"ut{parf}")
                tg_l = dma(TT[parf], d["tg"][128 * ti:128 * ti + 128, :], waits=wprev, slot=f"tt{parf}")
                f12g = dv(lambda h, parf=parf: h.tensor_mul(F12T[parf], UT[parf], TT[parf]),
                          waits=[ut_l, tg_l])
                tu_l = dma(TT[1 - parf], d["tu"][128 * ti:128 * ti + 128, :], waits=wprev, slot=f"tt{1 - parf}")
                f12u = dv(lambda h, parf=parf:
                          h.tensor_mul(F12T[1 - parf], UT[parf], TT[1 - parf]),
                          waits=[ut_l, tu_l, f12g])
            mtok = dma(MSK[gs % 2], d["marr"][128 * gs:128 * gs + 128, :],
                       waits=[x for x in (gsel, usel) if x is not None], slot=f"msk{gs % 2}")
            extra = [mtok]
            if s4 == 0 and gu_free[par_b] is not None:
                extra.append(gu_free[par_b])
            gview = WGB[par_b].rearrange("p (dc i) -> p i dc", i=128)[:, 32 * s4:32 * s4 + 32, :]
            gsel = recon_sub(gs % 3, parf, f12g, gview, extra, usel, MSK[gs % 2])
            uview = WUB[par_b].rearrange("p (dc i) -> p i dc", i=128)[:, 32 * s4:32 * s4 + 32, :]
            usel = recon_sub(gs % 3, 1 - parf, f12u, uview, extra, gsel, MSK[gs % 2])
        sel_of[ib] = (gsel, usel)

    recon_ib(0)
    for ib in range(64):
        par_b = ib % 2
        if ib + 1 < 64:
            recon_ib(ib + 1)
        gsel_ib, usel_ib = sel_of[ib]
        mm_g_last, mm_u_last = None, None
        for dc in range(16):
            for th in range(2):
                if dc == 0 and th == 0:
                    w = [gsel_ib, usel_ib, t_ln2]
                    if silu_toks:
                        w.append(silu_toks[-1])
                    if inter_done:
                        w.append(inter_done[-1])
                else:
                    w = []
                mm_g_last = te(lambda h, dc=dc, th=th, par_b=par_b:
                               h.matmul(ps_f[th][:], WGB[par_b][:, dc * 128:dc * 128 + 128],
                                        LN2FM[:, dc * 1024 + 512 * th:dc * 1024 + 512 * th + 512],
                                        start=(dc == 0), stop=(dc == 15)), waits=w)
            for th in range(2):
                mm_u_last = te(lambda h, dc=dc, th=th, par_b=par_b:
                               h.matmul(ps_f[2 + th][:], WUB[par_b][:, dc * 128:dc * 128 + 128],
                                        LN2FM[:, dc * 1024 + 512 * th:dc * 1024 + 512 * th + 512],
                                        start=(dc == 0), stop=(dc == 15)), waits=[])
        gu_free[par_b] = mm_u_last
        for th in range(2):
            w = [mm_g_last, mm_u_last]
            if inter_done:
                w.append(inter_done[-1])
            sl1 = ac(lambda h, th=th: h.activation(SIL, ps_f[th][:], AF.Silu), waits=w)
            silu_toks.append(sl1)
            mlt = dv(lambda h, th=th, ib=ib:
                     h.tensor_mul(INTER_SB[:, ib * 1024 + 512 * th:ib * 1024 + 512 * th + 512],
                                  SIL, ps_f[2 + th][:]), waits=[sl1])
            inter_done.append(mlt)
    t_inter = inter_done[-1]

    if DEBUG:
        dma_out(dbg["dbg_inter"], INTER_SB, waits=[t_inter])
    barrier()

    wd_free = [None, None]
    dev_free = [None, None]
    for hc in range(16):
        w = [t_wd_written]
        if wd_free[hc % 2] is not None:
            w.append(wd_free[hc % 2])
        trd = dma(WDRD[hc % 2], wd_hbm[:, hc * 8192:hc * 8192 + 8192], waits=w, slot=f"wdrd{hc % 2}")
        mm_last = [None, None]
        for ib in range(64):
            for th in range(2):
                if ib == 0:
                    wmm = [trd, t_inter]
                    if dev_free[th] is not None:
                        wmm.append(dev_free[th])
                else:
                    wmm = []
                mm_last[th] = te(lambda h, hc=hc, ib=ib, th=th:
                                 h.matmul(ps_f[2 * (hc % 2) + th][:],
                                          WDRD[hc % 2][:, ib * 128:ib * 128 + 128],
                                          INTER_SB[:, ib * 1024 + 512 * th:ib * 1024 + 512 * th + 512],
                                          start=(ib == 0), stop=(ib == 63)), waits=wmm)
        wd_free[hc % 2] = mm_last[1]
        for th in range(2):
            ev = ac(lambda h, hc=hc, th=th:
                    h.activation(DEV[th], ps_f[2 * (hc % 2) + th][:], AF.Copy),
                    waits=[mm_last[th]])
            od = dma_out(ffn_hbm[128 * hc:128 * hc + 128, 512 * th:512 * th + 512],
                         DEV[th], waits=[ev], slot=f"dev{th}")
            dev_free[th] = od

    barrier()

    prev_end = None
    for qt in range(8):
        w = [prev_end] if prev_end else []
        tf = p.emit(E_SY, lambda h, qt=qt: h.dma_start(
            out=FT8, in_=ffn_hbm[:, 128 * qt:128 * qt + 128], transpose=True),
            inc=(_slot("ft8"), 16), waits=w)
        tx = dma(XB[0], x1_hbm[128 * qt:128 * qt + 128, :], waits=w, slot="xb0")
        s10 = ln_stats(FT8, [tf] + w)
        a1 = dv(lambda h: h.tensor_scalar(X8A, FT8, sc_rstd, sc_nb,
                                          ALU.mult, ALU.add), waits=[s10])
        a2 = dv(lambda h: h.tensor_add(X8B, X8A, XB[0]), waits=[a1, tx])
        prev_end = a2
        dma_out(out_h[128 * qt:128 * qt + 128, :], X8B, waits=[a2], slot="outh")

    barrier()

    with es:
        with nc.Block() as block:
            @block.sync
            def _(h):
                p.replay(E_SY, h)

            @block.tensor
            def _(h):
                p.replay(E_TE, h)

            @block.vector
            def _(h):
                p.replay(E_DV, h)

            @block.scalar
            def _(h):
                p.replay(E_AC, h)
    return nc


_NC_CACHE = {}


def kernel(**inputs):
    if "nc" not in _NC_CACHE:
        _NC_CACHE["nc"] = build_program()
    nc = _NC_CACHE["nc"]
    in_maps = [_core_inputs(inputs, cid // 4, cid % 4) for cid in range(8)]
    res = run_bass_kernel_spmd(nc, in_maps, core_ids=list(range(8)))
    out = np.zeros((B, N, D), dtype=np.float32)
    for cid in range(8):
        b, c = cid // 4, cid % 4
        o = np.asarray(res.results[cid]["out"])
        rows = ((np.arange(8) * 512)[:, None] + 128 * c
                + np.arange(128)[None, :]).reshape(-1)
        out[b, rows] = o
    return out


# version 05 (final)
# speedup vs baseline: 1.0778x; 1.0508x over previous
import os
import subprocess
import sys
import tempfile
from contextlib import ExitStack
from math import comb

import numpy as np
import ml_dtypes

import concourse.bass as bass
import concourse.mybir as mybir
from concourse.bass_utils import run_bass_kernel_spmd

F32 = mybir.dt.float32
BF16 = mybir.dt.bfloat16
U8 = mybir.dt.uint8
AF = mybir.ActivationFunctionType
ALU = mybir.AluOpType
AX = mybir.AxisListType
BF = ml_dtypes.bfloat16

B, N, D = 2, 4096, 2048
H, KVH, HD = 16, 4, 128
INTER = 8192
N_CTRL = 130156
NSAMP = INTER * D
NCOL = NSAMP // 128
NSUB = NCOL // 512
NTILE = (NSUB + 2) // 3
OWN = 1024
EXP_SHIFT = -12.0
VW = 136

_CONST = {}
DEBUG = False


def _rope_tables():
    if "rope" in _CONST:
        return _CONST["rope"]
    prog = r"""
import jax
jax.config.update('jax_platforms', 'cpu')
jax.config.update('jax_enable_x64', True)
import jax.numpy as jnp, numpy as np, sys
HD, N, base = 128, 4096, 10000.0
inv = 1.0 / base ** (jnp.arange(0, HD, 2, dtype=jnp.float32) / HD)
t = jnp.arange(N, dtype=jnp.float32)
fr = jnp.outer(t, inv)
emb = jnp.concatenate([fr, fr], -1)
np.asarray(jnp.cos(emb)).astype(np.float32).tofile(sys.argv[1] + ".cos")
np.asarray(jnp.sin(emb)).astype(np.float32).tofile(sys.argv[1] + ".sin")
"""
    with tempfile.TemporaryDirectory() as td:
        pth = os.path.join(td, "rope.npz")
        env = dict(os.environ)
        env.pop("JAX_PLATFORMS", None)
        r = subprocess.run([sys.executable, "-c", prog, pth], env=env,
                           capture_output=True)
        if r.returncode != 0:
            raise RuntimeError(f"rope subprocess failed: {r.stderr[-2000:]}")
        cos = np.fromfile(pth + ".cos", dtype=np.float32).reshape(N, HD)
        sin = np.fromfile(pth + ".sin", dtype=np.float32).reshape(N, HD)
    ss = sin.copy()
    ss[:, :64] *= -1.0
    _CONST["rope"] = (cos, ss)
    return _CONST["rope"]


def _spline_packed():
    if "packed" in _CONST:
        return _CONST["packed"]
    t = np.linspace(0.0, N_CTRL - 1.0, NSAMP, dtype=np.float64)
    i = np.clip(np.floor(t).astype(np.int64), 0, N_CTRL - 2)
    i2 = i.reshape(NCOL, 128)
    t2 = t.reshape(NCOL, 128)
    i_c = i2[:, 0]
    b_c = (i2 == i_c[:, None]).sum(axis=1).astype(np.int64)
    phi = t2[:, 0] - i_c
    delta = (N_CTRL - 1.0) / (NSAMP - 1.0)
    tap_idx = np.clip(i_c[:, None] + np.arange(-1, 4)[None, :], 0, N_CTRL - 1)
    beta = np.array([[0.0, -0.5, 1.0, -0.5],
                     [1.0, 0.0, -2.5, 1.5],
                     [0.0, 0.5, 2.0, -1.5],
                     [0.0, 0.0, -0.5, 0.5]])
    pr = np.arange(128, dtype=np.float64)
    G = np.zeros((4, 4, 128))
    for k in range(4):
        for m in range(4):
            for q in range(m, 4):
                G[m, k] += beta[k, q] * comb(q, m) * (pr * delta) ** (q - m)
    G = G.reshape(16, 128)
    u_pow = phi[:, None] ** np.arange(4)[None, :]
    v_pow = (phi[:, None] - 1.0) ** np.arange(4)[None, :]
    U = np.zeros((NTILE, 128, 512), dtype=np.float64)
    TIDX = np.zeros((NSUB, 32, 512), dtype=np.int64)
    M = np.zeros((NSUB, 128, 512), dtype=np.uint8)
    pcol = np.arange(128)
    for gs in range(NSUB):
        ti, sl = gs // 3, gs % 3
        cols = np.arange(512 * gs, 512 * gs + 512)
        for m in range(4):
            for k in range(4):
                U[ti, 32 * sl + m * 4 + k, :] = u_pow[cols, m]
                U[ti, 32 * sl + 16 + m * 4 + k, :] = v_pow[cols, m]
                TIDX[gs, m * 4 + k, :] = tap_idx[cols, k]
                TIDX[gs, 16 + m * 4 + k, :] = tap_idx[cols, k + 1]
        M[gs] = (pcol[:, None] < b_c[cols][None, :]).astype(np.uint8)
    GST = np.zeros((128, 256), dtype=np.float64)
    for sl in range(3):
        GST[32 * sl:32 * sl + 16, 0:128] = G
        GST[32 * sl + 16:32 * sl + 32, 128:256] = G
    _CONST["packed"] = (U.astype(BF).reshape(NTILE * 128, 512),
                        M.reshape(NSUB * 128, 512), GST.astype(BF), TIDX)
    return _CONST["packed"]


def _core_inputs(inputs, b, c):
    U, M, GST, TIDX = _spline_packed()
    cos, ss = _rope_tables()
    x = np.asarray(inputs["L0_attn_x"][b])
    order = [(c + s) % 4 for s in range(4)]
    xp = np.ascontiguousarray(x.reshape(8, 4, 128, D)[:, order].reshape(N, D))
    cosp = cos.reshape(8, 4, 128, HD)[:, order].reshape(N, HD)
    ssp = ss.reshape(8, 4, 128, HD)[:, order].reshape(N, HD)
    coskr = np.ascontiguousarray(np.tile(cosp, (1, KVH)))
    sskr = np.ascontiguousarray(np.tile(ssp, (1, KVH)))
    own_rows = ((np.arange(8) * 512)[:, None] + 128 * c
                + np.arange(128)[None, :]).reshape(-1)
    s = np.float64(HD) ** -0.5
    cosqr = np.ascontiguousarray(np.tile((cos[own_rows].astype(np.float64) * s).astype(np.float32), (1, H)))
    ssqr = np.ascontiguousarray(np.tile((ss[own_rows].astype(np.float64) * s).astype(np.float32), (1, H)))
    tri = np.zeros((128, 512), dtype=np.float32)
    kk = np.arange(128)[:, None]
    pq = np.arange(128)[None, :]
    tri[:, 0:128] = np.where(kk <= pq, 0.0, -1e9)
    for s in range(1, 4):
        tri[:, 128 * s:128 * s + 128] = 0.0 if (c + s) % 4 < c else -1e9
    taps = {}
    for nm, key in (("tg", "L0_ffn_gate_cp"), ("tu", "L0_ffn_up_cp"),
                    ("td", "L0_ffn_down_cp")):
        cp = np.asarray(inputs[key])
        T = np.zeros((NTILE, 128, 512), dtype=BF)
        for gs in range(NSUB):
            T[gs // 3, 32 * (gs % 3):32 * (gs % 3) + 32, :] = cp[TIDX[gs]].astype(BF)
        taps[nm] = T.reshape(NTILE * 128, 512)
    return {
        "xp": xp, "coskr": coskr, "sskr": sskr, "cosqr": cosqr, "ssqr": ssqr,
        "tri_t": tri, "ident": np.eye(128, dtype=BF), "gst": GST,
        "uarr": U, "marr": M, "tg": taps["tg"], "tu": taps["tu"], "td": taps["td"],
        "wq": np.ascontiguousarray(inputs["L0_attn_Wq"]),
        "wk": np.ascontiguousarray(inputs["L0_attn_Wk"]),
        "wv": np.ascontiguousarray(inputs["L0_attn_Wv"]),
        "wo": np.ascontiguousarray(inputs["L0_attn_Wo"]),
    }


class Prog:
    def __init__(self):
        self.items = []
        self.cnt = {}
        self.sems = {}
        self.last_wait = {}

    def emit(self, eng, fn, inc=None, waits=()):
        for tok in waits:
            self._wait(eng, tok)
        tok_out = None
        if inc is not None:
            sem, amt = inc
            self.cnt[sem] = self.cnt.get(sem, 0) + amt
            tok_out = (sem, self.cnt[sem])

            def wrapped(h, fn=fn, sem=sem, amt=amt):
                fn(h).then_inc(self.sems[sem], amt)
        else:
            def wrapped(h, fn=fn):
                fn(h)
        self.items.append((eng, wrapped))
        return tok_out

    def _wait(self, eng, tok):
        if tok is None:
            return
        sem, val = tok
        if self.last_wait.get((eng, sem), -1) >= val:
            return
        self.last_wait[(eng, sem)] = val

        def w(h, sem=sem, val=val):
            h.wait_ge(self.sems[sem], val)
        self.items.append((eng, w))

    def all_toks(self):
        return [(s, v) for s, v in self.cnt.items()]

    def replay(self, eng, handle):
        for e, fn in self.items:
            if e == eng:
                fn(handle)


def build_program():
    nc = bass.Bass("TRN2", target_bir_lowering=False)
    d = {}

    def din(name, shape, dt):
        d[name] = nc.dram_tensor(name, shape, dt, kind="ExternalInput").ap()

    din("xp", [N, D], F32)
    din("coskr", [N, 512], F32)
    din("sskr", [N, 512], F32)
    din("cosqr", [OWN, D], F32)
    din("ssqr", [OWN, D], F32)
    din("tri_t", [128, 512], F32)
    din("ident", [128, 128], BF16)
    din("gst", [128, 256], BF16)
    din("uarr", [NTILE * 128, 512], BF16)
    din("marr", [NSUB * 128, 512], U8)
    din("tg", [NTILE * 128, 512], BF16)
    din("tu", [NTILE * 128, 512], BF16)
    din("td", [NTILE * 128, 512], BF16)
    din("wq", [D, D], F32)
    din("wk", [D, 512], F32)
    din("wv", [D, 512], F32)
    din("wo", [D, D], F32)
    out_h = nc.dram_tensor("out", [OWN, D], F32, kind="ExternalOutput").ap()
    dbg = {}
    if DEBUG:
        for nm, shp, dt in (("dbg_kfm", [128, KVH * N], BF16),
                            ("dbg_vtm", [128, KVH * 32 * VW], BF16),
                            ("dbg_qfm", [128, H * OWN], BF16),
                            ("dbg_atm", [128, 8 * D], BF16),
                            ("dbg_ln2fm", [128, 16 * OWN], BF16),
                            ("dbg_inter", [128, 65536], BF16),
                            ("dbg_fmb", [128, 8192], BF16)):
            dbg[nm] = nc.dram_tensor(nm, shp, dt, kind="ExternalOutput").ap()
    _scr_kind = "ExternalOutput" if DEBUG else "Internal"
    wd_hbm = nc.dram_tensor("wd_hbm", [128, NCOL], BF16, kind=_scr_kind).ap()
    o_hbm = nc.dram_tensor("o_hbm", [OWN, D], F32, kind=_scr_kind).ap()
    x1_hbm = nc.dram_tensor("x1_hbm", [OWN, D], F32, kind=_scr_kind).ap()
    ffn_hbm = nc.dram_tensor("ffn_hbm", [D, OWN], BF16, kind=_scr_kind).ap()

    es = ExitStack()
    ABF = es.enter_context(nc.sbuf_tensor("abf", [128, 96256], BF16)).ap()
    AF32 = es.enter_context(nc.sbuf_tensor("af32", [128, 4736], F32)).ap()
    AU8 = es.enter_context(nc.sbuf_tensor("au8", [128, 1024], U8)).ap()
    ps_f = [es.enter_context(nc.psum_tensor(f"psf{i}", [128, 512], F32)).ap()
            for i in range(6)]
    ps_b = [es.enter_context(nc.psum_tensor(f"psb{i}", [128, 1024], BF16)).ap()
            for i in range(2)]

    KFM = ABF[:, 0:16384]
    VTM = ABF[:, 16384:33792]
    WKV = ABF[:, 33792:50176]
    QFM = ABF[:, 33792:50176]
    AFM = ABF[:, 33792:50176]
    FMB = ABF[:, 50176:58368]
    WQ4 = ABF[:, 50176:58368]
    LNT = ABF[:, 58368:60416]
    WOH = ABF[:, 50176:66560]
    OWNFM = ABF[:, 66560:82944]
    ATM = ABF[:, 66560:82944]
    LN2FM = ABF[:, 66560:82944]
    WDRD = [ABF[:, 66560:74752], ABF[:, 74752:82944]]
    DSTG = ABF[:, 0:8192]
    INTER_SB = ABF[:, 0:65536]
    CKT = [ABF[:, 82944 + i * 1024:82944 + (i + 1) * 1024].bitcast(F32) for i in range(2)]
    SKT = [ABF[:, 84992 + i * 1024:84992 + (i + 1) * 1024].bitcast(F32) for i in range(2)]
    KSW = ABF[:, 87040:88064].bitcast(F32)
    KRP = ABF[:, 88064:89088].bitcast(F32)
    KPB = ABF[:, 89088:89600]
    PTB = [ABF[:, 82944:83456], ABF[:, 83456:83968]]
    TT = [ABF[:, 82944 + i * 512:82944 + (i + 1) * 512] for i in range(2)]
    UT = [ABF[:, 83968 + i * 512:83968 + (i + 1) * 512] for i in range(2)]
    F12T = [ABF[:, 84992 + i * 512:84992 + (i + 1) * 512] for i in range(2)]
    WGB = [ABF[:, 86016 + i * 2048:86016 + (i + 1) * 2048] for i in range(2)]
    WUB = [ABF[:, 90112 + i * 2048:90112 + (i + 1) * 2048] for i in range(2)]
    DEV = [ABF[:, 94208 + i * 512:94208 + (i + 1) * 512] for i in range(2)]
    SIL = ABF[:, 95616:96128]
    X1T = ABF[:, 8192:12288].bitcast(F32)
    X1T2 = ABF[:, 12288:16384].bitcast(F32)
    EV4 = [X1T, X1T2]
    FT8 = ABF[:, 16384:18432]
    X8A = ABF[:, 18432:22528].bitcast(F32)
    X8B = ABF[:, 22528:26624].bitcast(F32)
    GSTS = ABF[:, 95232:95488]
    IDENT = ABF[:, 95488:95616]
    XB = [AF32[:, 0:2048], AF32[:, 2048:4096]]
    TRI = AF32[:, 4096:4608]
    STATS = AF32[:, 4608:4736]
    sc_sum = STATS[:, 0:1]
    sc_ssq = STATS[:, 1:2]
    sc_mean = STATS[:, 2:3]
    sc_var = STATS[:, 3:4]
    sc_r1 = STATS[:, 4:5]
    sc_rstd = STATS[:, 5:6]
    sc_nb = STATS[:, 6:7]
    sc_t = STATS[:, 7:8]
    sc_rs = [STATS[:, 8 + i:9 + i] for i in range(2)]
    sc_nm5 = STATS[:, 10:11]
    sc_shift = STATS[:, 11:12]
    MSK = [AU8[:, 0:512], AU8[:, 512:1024]]

    p = Prog()
    for s in ("te", "dv", "ac"):
        p.sems[s] = es.enter_context(nc.semaphore(s))
    _slot_sems = {}

    def _slot(name):
        if name not in _slot_sems:
            p.sems["s_" + name] = es.enter_context(nc.semaphore("s_" + name))
            _slot_sems[name] = "s_" + name
        return _slot_sems[name]

    E_SY, E_TE, E_DV, E_AC = "sync", "tensor", "vector", "scalar"

    def dma(dst, src, waits=(), slot="misc"):
        return p.emit(E_SY, lambda h, a=dst, b=src: h.dma_start(out=a, in_=b),
                      inc=(_slot(name=slot), 16), waits=waits)

    def dma_out(dst, src, waits=(), slot="out"):
        return p.emit(E_SY, lambda h, a=dst, b=src: h.dma_start(out=a, in_=b),
                      inc=(_slot(name=slot), 16), waits=waits)

    def te(fn, waits=()):
        return p.emit(E_TE, fn, inc=("te", 1), waits=waits)

    def dv(fn, waits=()):
        return p.emit(E_DV, fn, inc=("dv", 1), waits=waits)

    def ac(fn, waits=()):
        return p.emit(E_AC, fn, inc=("ac", 1), waits=waits)

    def barrier():
        toks = p.all_toks()
        for eng in (E_SY, E_TE, E_DV, E_AC):
            for t in toks:
                p._wait(eng, t)

    psb_free = [None, None]

    def transpose_evac(src_tile, dst_tile, bank, waits):
        w = list(waits) + [t_id]
        if psb_free[bank] is not None:
            w.append(psb_free[bank])
        tr = te(lambda h, b=bank, s=src_tile:
                h.transpose(ps_b[b][:, 0:128], s, IDENT), waits=w)
        ev = dv(lambda h, b=bank, o=dst_tile: h.tensor_copy(o, ps_b[b][:, 0:128]),
                waits=[tr])
        psb_free[bank] = ev
        return tr, ev

    t_id = dma(IDENT, d["ident"][:], slot="ident")
    t_gst = dma(GSTS, d["gst"][:], slot="gst")
    t_tri = dma(TRI, d["tri_t"][:], slot="tri")
    t_shift = dv(lambda h: h.memset(sc_shift, EXP_SHIFT))
    vt_v = VTM.rearrange("p (t w) -> p t w", w=VW)
    z0 = dv(lambda h: h.memset(vt_v[:, :, 128:VW], 0.0))
    t_ones = dv(lambda h: h.memset(vt_v[:, :, 128:129], 1.0), waits=[z0])

    prev = None
    for dc in range(16):
        w = [prev] if prev else []
        tl = dma(XB[dc % 2][:, 0:512], d["wk"][128 * dc:128 * dc + 128, :], waits=w, slot=f"xb{dc % 2}")
        dv(lambda h, dc=dc: h.tensor_copy(WKV[:, dc * 512:dc * 512 + 512],
                                          XB[dc % 2][:, 0:512]), waits=[tl])
        tl2 = dma(XB[dc % 2][:, 512:1024], d["wv"][128 * dc:128 * dc + 128, :], waits=[tl], slot=f"xb{dc % 2}")
        prev = dv(lambda h, dc=dc: h.tensor_copy(
            WKV[:, 8192 + dc * 512:8192 + dc * 512 + 512],
            XB[dc % 2][:, 512:1024]), waits=[tl2])
    t_wkv = prev

    def ln_tile(xt, src_rows, prev_apply=None, lnt_guard=None, extra=()):
        tl = dma(xt, src_rows, waits=[x for x in (prev_apply,) if x] + list(extra))
        w0 = [tl] + [x for x in (prev_apply, lnt_guard) if x]
        s1 = dv(lambda h: h.reduce_sum(sc_sum, xt, axis=AX.X), waits=w0)
        s2 = ac(lambda h: h.activation(LNT, xt, AF.Square, accum_out=sc_ssq), waits=w0)
        s3 = dv(lambda h: h.tensor_scalar_mul(sc_mean, sc_sum, 1.0 / D), waits=[s1])
        s4 = dv(lambda h: h.tensor_scalar_mul(sc_var, sc_ssq, 1.0 / D), waits=[s2])
        s5 = dv(lambda h: h.tensor_mul(sc_t, sc_mean, sc_mean), waits=[s3])
        s6 = dv(lambda h: h.tensor_sub(sc_var, sc_var, sc_t), waits=[s4, s5])
        s7 = dv(lambda h: h.tensor_scalar_add(sc_var, sc_var, 1e-5), waits=[s6])
        s8 = dv(lambda h: h.reciprocal(sc_r1, sc_var), waits=[s7])
        s9 = ac(lambda h: h.activation(sc_rstd, sc_r1, AF.Sqrt), waits=[s8])
        s10 = dv(lambda h: h.tensor_scalar_mul(sc_nb, sc_mean, -1.0), waits=[s9])
        s11 = dv(lambda h: h.tensor_scalar(LNT, xt, sc_nb, sc_rstd,
                                           ALU.add, ALU.mult), waits=[s10])
        return tl, s11

    prev_apply = None
    last_tr = None
    fm_consumed = None
    kfm_done = []
    vtm_done = []
    ckt_free = [None, None]
    skt_free = [None, None]
    kpb_free = None
    for tt in range(32):
        pc, sub = tt // 4, tt % 4
        xt = XB[tt % 2]
        tl, app = ln_tile(xt, d["xp"][128 * tt:128 * tt + 128, :],
                          prev_apply=prev_apply, lnt_guard=last_tr,
                          extra=([t_wkv] if tt < 2 else ()))
        prev_apply = app
        evs = []
        for dc in range(16):
            w = [app]
            if fm_consumed is not None:
                w.append(fm_consumed)
            tr, ev = transpose_evac(
                LNT[:, dc * 128:dc * 128 + 128],
                FMB[:, dc * 512 + 128 * sub:dc * 512 + 128 * sub + 128],
                dc % 2, w)
            evs.append(ev)
            last_tr = tr
        if sub < 3:
            continue
        for st in range(4):
            gt = 4 * pc + st
            pk = ps_f[2 * (st % 2)]
            pv = ps_f[2 * (st % 2) + 1]
            mm_last = None
            for dc in range(16):
                if dc == 0:
                    w = [t_wkv, evs[-1]]
                    if kfm_done:
                        w.append(kfm_done[-1])
                    if vtm_done:
                        w.append(vtm_done[-1])
                else:
                    w = []
                te(lambda h, pk=pk, dc=dc, st=st:
                   h.matmul(pk[:], FMB[:, dc * 512 + 128 * st:dc * 512 + 128 * st + 128],
                            WKV[:, dc * 512:dc * 512 + 512],
                            start=(dc == 0), stop=(dc == 15)), waits=w)
                mm_last = te(lambda h, pv=pv, dc=dc, st=st:
                             h.matmul(pv[:], FMB[:, dc * 512 + 128 * st:dc * 512 + 128 * st + 128],
                                      WKV[:, 8192 + dc * 512:8192 + dc * 512 + 512],
                                      start=(dc == 0), stop=(dc == 15)), waits=[])
            ck = dma(CKT[st % 2], d["coskr"][128 * gt:128 * gt + 128, :],
                     waits=[ckt_free[st % 2]] if ckt_free[st % 2] else [], slot=f"ck{st % 2}")
            sk = dma(SKT[st % 2], d["sskr"][128 * gt:128 * gt + 128, :],
                     waits=[skt_free[st % 2]] if skt_free[st % 2] else [], slot=f"sk{st % 2}")
            kv3 = lambda ap: ap.rearrange("p (g h w) -> p (g h) w", h=2, w=64)
            r1 = dv(lambda h, pk=pk: h.tensor_copy(
                kv3(KSW)[:, 0::2, :], kv3(pk[:])[:, 1::2, :]), waits=[mm_last])
            r2 = dv(lambda h, pk=pk: h.tensor_copy(
                kv3(KSW)[:, 1::2, :], kv3(pk[:])[:, 0::2, :]), waits=[r1])
            r3 = dv(lambda h, st=st: h.tensor_mul(KRP, KSW, SKT[st % 2]),
                    waits=[r2, sk])
            r4 = dv(lambda h, pk=pk, st=st: h.tensor_mul(KSW, pk[:], CKT[st % 2]),
                    waits=[ck, r3])
            skt_free[st % 2] = r3
            ckt_free[st % 2] = r4
            w5 = [r4]
            if kpb_free is not None:
                w5.append(kpb_free)
            r5 = dv(lambda h: h.tensor_add(KPB, KSW, KRP), waits=w5)
            for g in range(KVH):
                trk, evk = transpose_evac(
                    KPB[:, g * 128:g * 128 + 128],
                    KFM[:, g * 4096 + 128 * gt:g * 4096 + 128 * gt + 128],
                    g % 2, [r5])
                kfm_done.append(evk)
                kpb_free = trk
                evv = ac(lambda h, pv=pv, g=g, gt=gt:
                         h.activation(VTM[:, g * 32 * VW + gt * VW:g * 32 * VW + gt * VW + 128],
                                      pv[:, g * 128:g * 128 + 128], AF.Copy),
                         waits=[mm_last, t_ones])
                vtm_done.append(evv)
        fm_consumed = kfm_done[-1]

    barrier()
    if DEBUG:
        dma_out(dbg["dbg_kfm"], KFM)
        dma_out(dbg["dbg_vtm"], VTM)
        dma_out(dbg["dbg_fmb"], FMB)
        barrier()

    prev_apply = None
    last_tr = None
    own_evs = []
    for tt in range(8):
        xt = XB[tt % 2]
        tl, app = ln_tile(xt, d["xp"][512 * tt:512 * tt + 128, :],
                          prev_apply=prev_apply, lnt_guard=last_tr)
        prev_apply = app
        for dc in range(16):
            tr, ev = transpose_evac(
                LNT[:, dc * 128:dc * 128 + 128],
                OWNFM[:, dc * 1024 + tt * 128:dc * 1024 + tt * 128 + 128],
                dc % 2, [app])
            own_evs.append(ev)
            last_tr = tr
    t_own = own_evs[-1]

    qfm_done = []
    wq_c = []
    kpb_free = None
    for hq in range(4):
        for dc in range(16):
            w = []
            if len(wq_c) >= 2:
                w.append(wq_c[-2])
            elif prev_apply is not None:
                w.append(prev_apply)
            if hq > 0:
                w.append(("te", p.cnt["te"]))
            tl = dma(XB[dc % 2][:, 0:512],
                     d["wq"][128 * dc:128 * dc + 128, 512 * hq:512 * hq + 512],
                     waits=w, slot=f"xb{dc % 2}")
            cc = dv(lambda h, dc=dc: h.tensor_copy(WQ4[:, dc * 512:dc * 512 + 512],
                                                   XB[dc % 2][:, 0:512]), waits=[tl])
            wq_c.append(cc)
        for tt in range(8):
            pq = ps_f[tt % 2]
            mm_last = None
            for dc in range(16):
                if dc == 0:
                    w = [wq_c[-1], t_own]
                    if len(qfm_done) >= 8:
                        w.append(qfm_done[-8])
                else:
                    w = []
                mm_last = te(lambda h, pq=pq, dc=dc, tt=tt:
                             h.matmul(pq[:], OWNFM[:, dc * 1024 + tt * 128:dc * 1024 + tt * 128 + 128],
                                      WQ4[:, dc * 512:dc * 512 + 512],
                                      start=(dc == 0), stop=(dc == 15)), waits=w)
            ck = dma(CKT[tt % 2], d["cosqr"][128 * tt:128 * tt + 128, 512 * hq:512 * hq + 512],
                     waits=[ckt_free[tt % 2]] if ckt_free[tt % 2] else [], slot=f"ck{tt % 2}")
            sk = dma(SKT[tt % 2], d["ssqr"][128 * tt:128 * tt + 128, 512 * hq:512 * hq + 512],
                     waits=[skt_free[tt % 2]] if skt_free[tt % 2] else [], slot=f"sk{tt % 2}")
            kv3 = lambda ap: ap.rearrange("p (g h w) -> p (g h) w", h=2, w=64)
            r1 = dv(lambda h, pq=pq: h.tensor_copy(
                kv3(KSW)[:, 0::2, :], kv3(pq[:])[:, 1::2, :]), waits=[mm_last])
            r2 = dv(lambda h, pq=pq: h.tensor_copy(
                kv3(KSW)[:, 1::2, :], kv3(pq[:])[:, 0::2, :]), waits=[r1])
            r3 = dv(lambda h, tt=tt: h.tensor_mul(KRP, KSW, SKT[tt % 2]),
                    waits=[r2, sk])
            r4 = dv(lambda h, pq=pq, tt=tt: h.tensor_mul(KSW, pq[:], CKT[tt % 2]),
                    waits=[ck, r3])
            skt_free[tt % 2] = r3
            ckt_free[tt % 2] = r4
            w5 = [r4]
            if kpb_free is not None:
                w5.append(kpb_free)
            r5 = dv(lambda h: h.tensor_add(KPB, KSW, KRP), waits=w5)
            for gg in range(4):
                hh = 4 * hq + gg
                trq, evq = transpose_evac(
                    KPB[:, gg * 128:gg * 128 + 128],
                    QFM[:, hh * 1024 + tt * 128:hh * 1024 + tt * 128 + 128],
                    gg % 2, [r5])
                qfm_done.append(evq)
                kpb_free = trq

    barrier()
    if DEBUG:
        dma_out(dbg["dbg_qfm"], QFM)
        barrier()

    def load_wo_half(dh, gate_tok):
        prev_c = gate_tok
        for acx in range(16):
            t1 = dma(XB[acx % 2][:, 0:1024],
                     d["wo"][128 * acx:128 * acx + 128, 1024 * dh:1024 * dh + 1024],
                     waits=[prev_c] if prev_c else [], slot=f"xb{acx % 2}")
            prev_c = dv(lambda h, acx=acx: h.tensor_copy(
                WOH[:, acx * 1024:acx * 1024 + 1024], XB[acx % 2][:, 0:1024]),
                waits=[t1])
        return prev_c

    t_woh = load_wo_half(0, None)

    atm_done = []
    pt_free = [None, None]
    st_free = [None, None]
    av_free = [None, None]
    rs_free = [None, None]
    t_q = qfm_done[-1]
    t_k = kfm_done[-1]
    t_v = vtm_done[-1]
    for hh in range(H):
        g = hh // 4
        for qt in range(8):
            avb = ps_f[2 + (qt % 2)]
            av_last = None

            def emit_st(kb, g=g, hh=hh, qt=qt):
                sbk = ps_f[4 + (kb % 2)]
                mms = []
                for ci in range(4):
                    if ci == 0:
                        w = [t_q, t_k]
                        if st_free[kb % 2] is not None:
                            w.append(st_free[kb % 2])
                    else:
                        w = []
                    mms.append(te(lambda h, sbk=sbk, g=g, kb=kb, ci=ci, hh=hh, qt=qt:
                                  h.matmul(sbk[:, 128 * ci:128 * ci + 128],
                                           KFM[:, g * 4096 + 512 * kb + 128 * ci:
                                               g * 4096 + 512 * kb + 128 * ci + 128],
                                           QFM[:, hh * 1024 + qt * 128:hh * 1024 + qt * 128 + 128],
                                           start=True, stop=True), waits=w))
                if kb == qt:
                    mk = dv(lambda h, sbk=sbk: h.tensor_add(sbk[:], sbk[:], TRI),
                            waits=mms + [t_tri])
                    exp_w = [mk]
                else:
                    exp_w = [mms[-1]]
                if pt_free[kb % 2] is not None:
                    exp_w = exp_w + [pt_free[kb % 2]]
                ex = ac(lambda h, sbk=sbk, kb=kb:
                        h.activation(PTB[kb % 2], sbk[:], AF.Exp, bias=sc_shift),
                        waits=exp_w + [t_shift])
                st_free[kb % 2] = ex
                return ex

            ex_of = {0: emit_st(0)}
            for kb in range(qt + 1):
                if kb + 1 <= qt:
                    ex_of[kb + 1] = emit_st(kb + 1)
                ex = ex_of[kb]
                for ci in range(4):
                    w = [ex]
                    if kb == 0 and ci == 0:
                        w.append(t_v)
                        if av_free[qt % 2] is not None:
                            w.append(av_free[qt % 2])
                    av_last = te(lambda h, avb=avb, g=g, kb=kb, ci=ci:
                                 h.matmul(avb[:, 0:VW],
                                          PTB[kb % 2][:, 128 * ci:128 * ci + 128],
                                          VTM[:, g * 32 * VW + (4 * kb + ci) * VW:
                                              g * 32 * VW + (4 * kb + ci) * VW + VW],
                                          start=(kb == 0 and ci == 0),
                                          stop=(kb == qt and ci == 3)), waits=w)
                pt_free[kb % 2] = av_last
            rsl = sc_rs[qt % 2]
            w = [av_last]
            if rs_free[qt % 2] is not None:
                w.append(rs_free[qt % 2])
            rc = dv(lambda h, avb=avb, rsl=rsl: h.reciprocal(rsl, avb[:, 128:129]),
                    waits=w)
            evo = dv(lambda h, avb=avb, rsl=rsl, hh=hh, qt=qt:
                     h.tensor_scalar_mul(ATM[:, qt * 2048 + hh * 128:qt * 2048 + hh * 128 + 128],
                                         avb[:, 0:128], rsl), waits=[rc])
            rs_free[qt % 2] = evo
            av_free[qt % 2] = evo
            atm_done.append(evo)

    if DEBUG:
        dma_out(dbg["dbg_atm"], ATM, waits=[atm_done[-1]])
    barrier()

    def recon_sub(sl, fslot, f12_tok, sel_view, sel_waits, psum_guard, msk):
        w1w = [f12_tok, t_gst]
        if psum_guard is not None:
            w1w.append(psum_guard)
        w1 = te(lambda h, sl=sl, fslot=fslot:
                h.matmul(ps_f[4][:], GSTS[32 * sl:32 * sl + 32, 0:128],
                         F12T[fslot][32 * sl:32 * sl + 32, :],
                         start=True, stop=True, tile_position=(32 * sl, 0)),
                waits=w1w)
        te(lambda h, sl=sl, fslot=fslot:
           h.matmul(ps_f[5][:], GSTS[32 * sl:32 * sl + 32, 128:256],
                    F12T[fslot][32 * sl:32 * sl + 32, :],
                    start=True, stop=True, tile_position=(32 * sl, 0)), waits=[])
        t_w2 = ("te", p.cnt["te"])
        sel = dv(lambda h, v=sel_view, m=msk:
                 h.select(v, m, ps_f[4][:], ps_f[5][:]),
                 waits=[t_w2] + list(sel_waits))
        return sel

    down_sel = None
    stg_dma = None
    f12_tok = None
    for gs in range(NSUB):
        hc = gs // 16
        hl = 8 * (gs % 16)
        ti, parf = gs // 3, (gs // 3) % 2
        if gs % 3 == 0:
            wprev = [down_sel] if down_sel else []
            ut_l = dma(UT[parf], d["uarr"][128 * ti:128 * ti + 128, :], waits=wprev, slot=f"ut{parf}")
            tt_l = dma(TT[parf], d["td"][128 * ti:128 * ti + 128, :], waits=wprev, slot=f"tt{parf}")
            f12_tok = dv(lambda h, parf=parf: h.tensor_mul(F12T[parf], UT[parf], TT[parf]),
                         waits=[ut_l, tt_l])
        mtok = dma(MSK[gs % 2], d["marr"][128 * gs:128 * gs + 128, :],
                   waits=[down_sel] if down_sel else [], slot=f"msk{gs % 2}")
        sel_w = [mtok]
        if stg_dma is not None and gs % 16 == 0:
            sel_w.append(stg_dma)
        view = DSTG.rearrange("p (ic hl) -> p hl ic", hl=128)[:, hl:hl + 8, :]
        down_sel = recon_sub(gs % 3, parf, f12_tok, view, sel_w, down_sel, MSK[gs % 2])
        if gs % 16 == 15:
            stg_dma = dma_out(wd_hbm[:, hc * 8192:hc * 8192 + 8192], DSTG,
                              waits=[down_sel], slot="stg")
    t_wd_written = stg_dma

    afm_done = []
    t_atm = atm_done[-1]
    for qt in range(8):
        for acx in range(16):
            tr, ev = transpose_evac(
                ATM[:, qt * 2048 + acx * 128:qt * 2048 + acx * 128 + 128],
                AFM[:, (qt * 16 + acx) * 128:(qt * 16 + acx) * 128 + 128],
                acx % 2, [t_atm])
            afm_done.append(ev)
    t_afm = afm_done[-1]

    o_free = [None] * 4
    for dh in range(2):
        if dh == 1:
            t_woh = load_wo_half(1, ("te", p.cnt["te"]))
        for qt in range(8):
            pso = [ps_f[2 * (qt % 2)], ps_f[2 * (qt % 2) + 1]]
            mm_last = None
            for acx in range(16):
                for db in range(2):
                    if acx == 0 and db == 0:
                        w = [t_afm, t_woh]
                        for b in (0, 1):
                            if o_free[2 * (qt % 2) + b] is not None:
                                w.append(o_free[2 * (qt % 2) + b])
                    else:
                        w = []
                    mm_last = te(lambda h, pso=pso, qt=qt, acx=acx, db=db:
                                 h.matmul(pso[db][:],
                                          AFM[:, (qt * 16 + acx) * 128:(qt * 16 + acx) * 128 + 128],
                                          WOH[:, acx * 1024 + 512 * db:acx * 1024 + 512 * db + 512],
                                          start=(acx == 0), stop=(acx == 15)), waits=w)
            for db in range(2):
                evd = dv(lambda h, pso=pso, db=db, qt=qt:
                         h.tensor_copy(EV4[qt % 2][:, 512 * db:512 * db + 512], pso[db][:]),
                         waits=[mm_last])
                od = dma_out(o_hbm[128 * qt:128 * qt + 128,
                                   1024 * dh + 512 * db:1024 * dh + 512 * db + 512],
                             EV4[qt % 2][:, 512 * db:512 * db + 512], waits=[evd],
                             slot=f"oev{2 * (qt % 2) + db}")
                o_free[2 * (qt % 2) + db] = od

    barrier()

    def ln_stats(src, w0):
        s1 = dv(lambda h: h.reduce_sum(sc_sum, src, axis=AX.X), waits=w0)
        s2 = ac(lambda h: h.activation(LNT, src, AF.Square, accum_out=sc_ssq), waits=w0)
        s3 = dv(lambda h: h.tensor_scalar_mul(sc_mean, sc_sum, 1.0 / D), waits=[s1])
        s4 = dv(lambda h: h.tensor_scalar_mul(sc_var, sc_ssq, 1.0 / D), waits=[s2])
        s5 = dv(lambda h: h.tensor_mul(sc_t, sc_mean, sc_mean), waits=[s3])
        s6 = dv(lambda h: h.tensor_sub(sc_var, sc_var, sc_t), waits=[s4, s5])
        s7 = dv(lambda h: h.tensor_scalar_add(sc_var, sc_var, 1e-5), waits=[s6])
        s8 = dv(lambda h: h.reciprocal(sc_r1, sc_var), waits=[s7])
        s9 = ac(lambda h: h.activation(sc_rstd, sc_r1, AF.Sqrt), waits=[s8])
        s10a = dv(lambda h: h.scalar_tensor_tensor(sc_nb, sc_mean, -1.0, sc_rstd,
                                                   ALU.mult, ALU.mult), waits=[s9])
        s10 = dv(lambda h: h.tensor_scalar_mul(sc_nm5, sc_mean, -1.0), waits=[s10a])
        return s10

    prev_end = None
    last_tr = None
    ln2_done = []
    for qt in range(8):
        w = [prev_end] if prev_end else []
        to = dma(XB[0], o_hbm[128 * qt:128 * qt + 128, :], waits=w, slot="xb0")
        tx = dma(XB[1], d["xp"][512 * qt:512 * qt + 128, :], waits=w, slot="xb1")
        w0 = [to] + ([prev_end] if prev_end else []) + ([last_tr] if last_tr else [])
        s10 = ln_stats(XB[0], w0)
        a1 = dv(lambda h: h.tensor_scalar(X1T, XB[0], sc_rstd, sc_nb,
                                          ALU.mult, ALU.add), waits=[s10])
        a2 = dv(lambda h: h.tensor_add(X1T2, X1T, XB[1]), waits=[a1, tx])
        sp = dma_out(x1_hbm[128 * qt:128 * qt + 128, :], X1T2, waits=[a2], slot="x1sp")
        s10b = ln_stats(X1T2, [a2])
        appb = dv(lambda h: h.tensor_scalar(LNT, X1T2, sc_nm5, sc_rstd,
                                            ALU.add, ALU.mult), waits=[s10b, sp])
        prev_end = appb
        for dc in range(16):
            tr, ev = transpose_evac(
                LNT[:, dc * 128:dc * 128 + 128],
                LN2FM[:, dc * 1024 + qt * 128:dc * 1024 + qt * 128 + 128],
                dc % 2, [appb])
            ln2_done.append(ev)
            last_tr = tr
        prev_end = ev
    t_ln2 = ln2_done[-1]

    if DEBUG:
        dma_out(dbg["dbg_ln2fm"], LN2FM, waits=[t_ln2])
    barrier()

    gsel, usel = None, None
    f12g, f12u = None, None
    cur_tile = -1
    gu_free = [None, None]
    silu_toks = []
    inter_done = []
    sel_of = {}

    def recon_ib(ib):
        nonlocal gsel, usel, f12g, f12u, cur_tile
        par_b = ib % 2
        for s4 in range(4):
            gs = 4 * ib + s4
            ti, parf = gs // 3, (gs // 3) % 2
            if ti != cur_tile:
                cur_tile = ti
                wprev = [x for x in (gsel, usel) if x is not None]
                ut_l = dma(UT[parf], d["uarr"][128 * ti:128 * ti + 128, :], waits=wprev, slot=f"ut{parf}")
                tg_l = dma(TT[parf], d["tg"][128 * ti:128 * ti + 128, :], waits=wprev, slot=f"tt{parf}")
                f12g = dv(lambda h, parf=parf: h.tensor_mul(F12T[parf], UT[parf], TT[parf]),
                          waits=[ut_l, tg_l])
                tu_l = dma(TT[1 - parf], d["tu"][128 * ti:128 * ti + 128, :], waits=wprev, slot=f"tt{1 - parf}")
                f12u = dv(lambda h, parf=parf:
                          h.tensor_mul(F12T[1 - parf], UT[parf], TT[1 - parf]),
                          waits=[ut_l, tu_l, f12g])
            mtok = dma(MSK[gs % 2], d["marr"][128 * gs:128 * gs + 128, :],
                       waits=[x for x in (gsel, usel) if x is not None], slot=f"msk{gs % 2}")
            extra = [mtok]
            if s4 == 0 and gu_free[par_b] is not None:
                extra.append(gu_free[par_b])
            gview = WGB[par_b].rearrange("p (dc i) -> p i dc", i=128)[:, 32 * s4:32 * s4 + 32, :]
            gsel = recon_sub(gs % 3, parf, f12g, gview, extra, usel, MSK[gs % 2])
            uview = WUB[par_b].rearrange("p (dc i) -> p i dc", i=128)[:, 32 * s4:32 * s4 + 32, :]
            usel = recon_sub(gs % 3, 1 - parf, f12u, uview, extra, gsel, MSK[gs % 2])
        sel_of[ib] = (gsel, usel)

    recon_ib(0)
    for ib in range(64):
        par_b = ib % 2
        if ib + 1 < 64:
            recon_ib(ib + 1)
        gsel_ib, usel_ib = sel_of[ib]
        mm_g_last, mm_u_last = None, None
        for dc in range(16):
            for th in range(2):
                if dc == 0 and th == 0:
                    w = [gsel_ib, usel_ib, t_ln2]
                    if silu_toks:
                        w.append(silu_toks[-1])
                    if inter_done:
                        w.append(inter_done[-1])
                else:
                    w = []
                mm_g_last = te(lambda h, dc=dc, th=th, par_b=par_b:
                               h.matmul(ps_f[th][:], WGB[par_b][:, dc * 128:dc * 128 + 128],
                                        LN2FM[:, dc * 1024 + 512 * th:dc * 1024 + 512 * th + 512],
                                        start=(dc == 0), stop=(dc == 15)), waits=w)
            for th in range(2):
                mm_u_last = te(lambda h, dc=dc, th=th, par_b=par_b:
                               h.matmul(ps_f[2 + th][:], WUB[par_b][:, dc * 128:dc * 128 + 128],
                                        LN2FM[:, dc * 1024 + 512 * th:dc * 1024 + 512 * th + 512],
                                        start=(dc == 0), stop=(dc == 15)), waits=[])
        gu_free[par_b] = mm_u_last
        for th in range(2):
            w = [mm_g_last, mm_u_last]
            if inter_done:
                w.append(inter_done[-1])
            sl1 = ac(lambda h, th=th: h.activation(SIL, ps_f[th][:], AF.Silu), waits=w)
            silu_toks.append(sl1)
            mlt = dv(lambda h, th=th, ib=ib:
                     h.tensor_mul(INTER_SB[:, ib * 1024 + 512 * th:ib * 1024 + 512 * th + 512],
                                  SIL, ps_f[2 + th][:]), waits=[sl1])
            inter_done.append(mlt)
    t_inter = inter_done[-1]

    if DEBUG:
        dma_out(dbg["dbg_inter"], INTER_SB, waits=[t_inter])
    barrier()

    wd_free = [None, None]
    dev_free = [None, None]
    for hc in range(16):
        w = [t_wd_written]
        if wd_free[hc % 2] is not None:
            w.append(wd_free[hc % 2])
        trd = dma(WDRD[hc % 2], wd_hbm[:, hc * 8192:hc * 8192 + 8192], waits=w, slot=f"wdrd{hc % 2}")
        mm_last = [None, None]
        for ib in range(64):
            for th in range(2):
                if ib == 0:
                    wmm = [trd, t_inter]
                    if dev_free[th] is not None:
                        wmm.append(dev_free[th])
                else:
                    wmm = []
                mm_last[th] = te(lambda h, hc=hc, ib=ib, th=th:
                                 h.matmul(ps_f[2 * (hc % 2) + th][:],
                                          WDRD[hc % 2][:, ib * 128:ib * 128 + 128],
                                          INTER_SB[:, ib * 1024 + 512 * th:ib * 1024 + 512 * th + 512],
                                          start=(ib == 0), stop=(ib == 63)), waits=wmm)
        wd_free[hc % 2] = mm_last[1]
        for th in range(2):
            ev = ac(lambda h, hc=hc, th=th:
                    h.activation(DEV[th], ps_f[2 * (hc % 2) + th][:], AF.Copy),
                    waits=[mm_last[th]])
            od = dma_out(ffn_hbm[128 * hc:128 * hc + 128, 512 * th:512 * th + 512],
                         DEV[th], waits=[ev], slot=f"dev{th}")
            dev_free[th] = od

    barrier()

    prev_end = None
    for qt in range(8):
        w = [prev_end] if prev_end else []
        tf = p.emit(E_SY, lambda h, qt=qt: h.dma_start(
            out=FT8, in_=ffn_hbm[:, 128 * qt:128 * qt + 128], transpose=True),
            inc=(_slot("ft8"), 16), waits=w)
        tx = dma(XB[0], x1_hbm[128 * qt:128 * qt + 128, :], waits=w, slot="xb0")
        s10 = ln_stats(FT8, [tf] + w)
        a1 = dv(lambda h: h.tensor_scalar(X8A, FT8, sc_rstd, sc_nb,
                                          ALU.mult, ALU.add), waits=[s10])
        a2 = dv(lambda h: h.tensor_add(X8B, X8A, XB[0]), waits=[a1, tx])
        prev_end = a2
        dma_out(out_h[128 * qt:128 * qt + 128, :], X8B, waits=[a2], slot="outh")

    barrier()

    with es:
        with nc.Block() as block:
            @block.sync
            def _(h):
                p.replay(E_SY, h)

            @block.tensor
            def _(h):
                p.replay(E_TE, h)

            @block.vector
            def _(h):
                p.replay(E_DV, h)

            @block.scalar
            def _(h):
                p.replay(E_AC, h)
    return nc


_NC_CACHE = {}


def kernel(**inputs):
    if "nc" not in _NC_CACHE:
        _NC_CACHE["nc"] = build_program()
    nc = _NC_CACHE["nc"]
    in_maps = [_core_inputs(inputs, cid // 4, cid % 4) for cid in range(8)]
    res = run_bass_kernel_spmd(nc, in_maps, core_ids=list(range(8)))
    out = np.zeros((B, N, D), dtype=np.float32)
    for cid in range(8):
        b, c = cid // 4, cid % 4
        o = np.asarray(res.results[cid]["out"])
        rows = ((np.arange(8) * 512)[:, None] + 128 * c
                + np.arange(128)[None, :]).reshape(-1)
        out[b, rows] = o
    return out
